# version 07
# baseline (speedup 1.0000x reference)
import numpy as np
import ml_dtypes
import concourse.bass as bass
import concourse.mybir as mybir
from concourse.bass_utils import run_bass_kernel_spmd

F32 = mybir.dt.float32
BF16 = mybir.dt.bfloat16
I32 = mybir.dt.int32
U8 = mybir.dt.uint8
AF = mybir.ActivationFunctionType
ALU = mybir.AluOpType
AX = mybir.AxisListType

B = 2
T = 2048
D = 1024
H = 16
DH = 64
TT = T // 128
NCH = D // 128
QB = 512
NQB = T // QB
L0H = 3072
NH0 = L0H // 128
L1H = 4096
NH1 = L1H // 128
NEXP = 8
EPS = 1e-5
EPSN = 1e-6

POOL_BYTES = 210944


class Sem:
    def __init__(self, h):
        self.h = h
        self.n = 0


class Prog:

    def __init__(self, nc):
        self.nc = nc
        self.q = {e: [] for e in ("tensor", "scalar", "vector", "sync", "gpsimd")}
        self.self_sems = {}

    def _push(self, eng, fn, wait, inc, inc_k, self_wait=True):
        waits = []
        ss = self.self_sems.get(eng)
        if ss is not None and self_wait:
            waits.append((ss.h, ss.n))
        for w in wait:
            if isinstance(w, tuple):
                waits.append((w[0].h, w[1]))
            else:
                waits.append((w.h, w.n))
        if inc is None and ss is not None:
            inc = ss
            inc_k = 1
        if inc is False:
            inc = None
        if inc is not None:
            inc.n += inc_k
            incs = (inc.h, inc_k)
        else:
            incs = None

        def play(e):
            for h, v in waits:
                if v > 0:
                    e.wait_ge(h, v)
            ins = fn(e)
            if incs is not None:
                ins.then_inc(*incs)
        self.q[eng].append(play)

    def te(self, fn, wait=(), inc=None, inc_k=1):
        self._push("tensor", fn, wait, inc, inc_k)

    def sc(self, fn, wait=(), inc=None, inc_k=1, self_wait=True):
        self._push("scalar", fn, wait, inc, inc_k, self_wait)

    def ve(self, fn, wait=(), inc=None, inc_k=1):
        self._push("vector", fn, wait, inc, inc_k)

    def sy(self, fn, wait=(), inc=None, inc_k=16):
        self._push("sync", fn, wait, inc, inc_k)

    def gp(self, fn, wait=(), inc=None, inc_k=16):
        self._push("gpsimd", fn, wait, inc, inc_k)

    def play(self):
        with self.nc.Block() as blk:
            @blk.tensor
            def _(e):
                for f in self.q["tensor"]:
                    f(e)

            @blk.scalar
            def _(e):
                for f in self.q["scalar"]:
                    f(e)

            @blk.vector
            def _(e):
                for f in self.q["vector"]:
                    f(e)

            @blk.sync
            def _(e):
                for f in self.q["sync"]:
                    f(e)

            @blk.gpsimd
            def _(e):
                for f in self.q["gpsimd"]:
                    f(e)


class SBAlloc:

    def __init__(self, nc, pool_bytes):
        self.nc = nc
        self.guard = nc.sbuf_tensor("pool", [128, pool_bytes], U8)
        self.pool = self.guard.__enter__()
        self.base = nc.lookup_mloc(self.pool).addr
        self.off = 0
        self.lim = pool_bytes
        self.stack = []
        self.uid = 0

    def alloc(self, name, shape, dtype):
        self.uid += 1
        name = f"{name}_{self.uid}"
        h = self.nc.virtual_sbuf_tensor(name, list(shape), dtype)
        per = int(np.prod(shape[1:])) * mybir.dt.size(dtype)
        addr = (self.off + 31) & ~31
        self.off = addr + per
        assert self.off <= self.lim, \
            f"SBUF pool overflow: {name} needs {per}B at {addr}, lim {self.lim}"
        self.nc._tensor(h.name, list(h.shape), h.dtype, type="SB")
        m = self.nc.lookup_mloc(h)
        m.allocated = True
        m.addr = self.base + addr
        return h

    def push(self):
        self.stack.append(self.off)

    def pop(self):
        self.off = self.stack.pop()


def build(debug=False):
    nc = bass.Bass("TRN2", target_bir_lowering=False, detect_race_conditions=False)

    x_in = nc.dram_tensor("x", [B, T, D], F32, kind="ExternalInput").ap()
    y_out = nc.dram_tensor("y", [B, T, D], F32, kind="ExternalOutput").ap()
    xmid = nc.dram_tensor("xmid", [B, T, D], F32, kind="Internal").ap()
    ssp = nc.dram_tensor("ssp", [B, T, D], BF16, kind="Internal").ap()
    xnsp = nc.dram_tensor("xnsp", [B, 128, NCH * T], BF16, kind="Internal").ap()
    if debug:
        DBGK = "ExternalOutput"
        dbg_xnt0 = nc.dram_tensor("dbg_xnt0", [B, 128, NCH * T], BF16, kind=DBGK).ap()
        dbg_s0 = nc.dram_tensor("dbg_s0", [B, T, D], BF16, kind=DBGK).ap()
        dbg_kt0 = nc.dram_tensor("dbg_kt0", [B, 128, NCH * T], BF16, kind=DBGK).ap()
        dbg_v0 = nc.dram_tensor("dbg_v0", [B, 128, TT * H * (DH + 1)], BF16, kind=DBGK).ap()
        dbg_qt0 = nc.dram_tensor("dbg_qt0", [B, 128, NCH * QB], BF16, kind=DBGK).ap()
        dbg_at0 = nc.dram_tensor("dbg_at0", [B, 128, H * QB], BF16, kind=DBGK).ap()
        dbg_g = nc.dram_tensor("dbg_g", [B, T, D], BF16, kind=DBGK).ap()
        dbg_h = nc.dram_tensor("dbg_h", [128, 256], BF16, kind=DBGK).ap()
        dbg_sg = nc.dram_tensor("dbg_sg", [128, 256], BF16, kind=DBGK).ap()
        dbg_xmid = nc.dram_tensor("dbg_xmid", [B, T, D], F32, kind=DBGK).ap()
        dbg_st = nc.dram_tensor("dbg_st", [128, 6 * TT], F32, kind=DBGK).ap()
        dbg_ub = nc.dram_tensor("dbg_ub", [128, 3 * D], BF16, kind=DBGK).ap()
        dbg_qsc = nc.dram_tensor("dbg_qsc", [128, D], F32, kind=DBGK).ap()
        dbg_rq = nc.dram_tensor("dbg_rq", [128, 4], F32, kind=DBGK).ap()

    wq = {}
    for l in (0, 1):
        for nm in ("qwT", "kwT", "vwT", "owT"):
            wq[(l, nm)] = nc.dram_tensor(f"l{l}_{nm}", [D, D], F32, kind="ExternalInput").ap()
        for nm in ("g_q", "g_k", "g_v", "b_q", "b_k", "b_v"):
            wq[(l, nm)] = nc.dram_tensor(f"l{l}_{nm}", [128, NCH], F32, kind="ExternalInput").ap()
        for nm in ("qb", "kb", "vb", "ob"):
            wq[(l, nm)] = nc.dram_tensor(f"l{l}_{nm}", [1, D], F32, kind="ExternalInput").ap()
        wq[(l, "beta")] = nc.dram_tensor(f"l{l}_beta", [1, H], F32, kind="ExternalInput").ap()
    wg_t = nc.dram_tensor("wg_t", [D, L0H], I32, kind="ExternalInput").ap()
    wu_t = nc.dram_tensor("wu_t", [D, L0H], I32, kind="ExternalInput").ap()
    wd_t = nc.dram_tensor("wd_t", [L0H, D], I32, kind="ExternalInput").ap()
    rwg = nc.dram_tensor("rwg", [D, NEXP], F32, kind="ExternalInput").ap()
    rbg = nc.dram_tensor("rbg", [1, NEXP], F32, kind="ExternalInput").ap()
    w1 = nc.dram_tensor("w1", [NEXP, D, L1H], F32, kind="ExternalInput").ap()
    b1 = nc.dram_tensor("b1", [128, NEXP * NH1], F32, kind="ExternalInput").ap()
    w2 = nc.dram_tensor("w2", [NEXP, L1H, D], F32, kind="ExternalInput").ap()
    b2 = nc.dram_tensor("b2", [1, NEXP * D], F32, kind="ExternalInput").ap()

    P = Prog(nc)
    ctxs = []

    def mksem(name):
        g = nc.semaphore(name)
        h = g.__enter__()
        ctxs.append(g)
        return Sem(h)

    SPE = mksem("spe")
    SACT = mksem("sact")
    SDVE = mksem("sdve")
    SGPC = mksem("sgpc")
    SSY = mksem("ssy")
    DW = mksem("dw")
    DX = mksem("dx")
    DS = mksem("ds")
    P.self_sems = {"vector": SDVE, "scalar": SACT}

    def barrier():
        def mkinc(S):
            def f(e):
                e.wait_ge(S.h, 0) if False else None
                return e.sem_inc(S.h, 1)
            return f
        P.sy(lambda e: e.sem_inc(SSY.h, 1), wait=[(DX, DX.n), (DS, DS.n)], inc=False)
        SSY.n += 1
        P.gp(lambda e: (e.drain(fusable=False), e.sem_inc(SGPC.h, 1))[1], wait=[(DW, DW.n)], inc=False)
        SGPC.n += 1
        P.te(lambda e: (e.drain(fusable=False), e.sem_inc(SPE.h, 1))[1], inc=False)
        SPE.n += 1
        P.sc(lambda e: (e.drain(fusable=False), e.sem_inc(SACT.h, 1))[1], inc=False)
        SACT.n += 1
        P.ve(lambda e: (e.drain(fusable=False), e.sem_inc(SDVE.h, 1))[1], inc=False)
        SDVE.n += 1
        sems = [SPE, SACT, SDVE, SGPC, SSY]
        for emitfn in (P.sy, P.gp, P.te, P.sc, P.ve):
            emitfn(lambda e: e.nop(), wait=list(sems), inc=False)

    def dxv():
        return (DX, DX.n)

    def dwv():
        return (DW, DW.n)

    psg = nc.psum_tensor("ps", [128, 4096], F32)
    ps = psg.__enter__()
    ctxs.append(psg)

    def bank(i, n=1):
        return ps[:, i * 512:(i + n) * 512]

    A = SBAlloc(nc, POOL_BYTES)
    ident = A.alloc("ident", [128, 128], BF16)
    ones_bf = A.alloc("ones_bf", [128, 128], BF16)
    ones32 = A.alloc("ones32", [128, 128], F32)
    eps_c = A.alloc("eps_c", [128, 1], F32)
    trash = A.alloc("trash", [128, 1024], BF16)
    st_scale = A.alloc("st_scale", [128, TT], F32)
    st_bias = A.alloc("st_bias", [128, TT], F32)
    st_r2 = A.alloc("st_r2", [128, TT], F32)
    st_tmp = A.alloc("st_tmp", [128, 4 * TT], F32)
    sums = A.alloc("sums", [128, TT], F32)
    sumsq = A.alloc("sumsq", [128, TT], F32)
    rk_rec = A.alloc("rk_rec", [128, TT], F32)
    beta8 = A.alloc("beta8", [128, H], F32)
    gam = A.alloc("gam", [128, 3 * NCH], F32)
    betv = A.alloc("betv", [128, 3 * NCH], F32)
    betv_bf = A.alloc("betv_bf", [128, 3 * NCH], BF16)
    w_moe = A.alloc("w_moe", [128, 2 * TT * NEXP], F32)
    b1_sb = A.alloc("b1_sb", [128, NEXP * NH1], F32)
    bgrow = A.alloc("bgrow", [1, NEXP], BF16)
    rwg_sb = A.alloc("rwg_sb", [128, NCH * NEXP], BF16)
    betarow = A.alloc("betarow", [1, H], F32)

    def emit_setup():
        io1 = trash[:].bitcast(F32)[:, 0:128]
        io2 = trash[:].bitcast(F32)[:, 128:129]
        P.gp(lambda e: e.iota(io1, [[1, 128]], channel_multiplier=0,
                              allow_small_or_imprecise_dtypes=True), inc=SGPC, inc_k=1)
        P.gp(lambda e: e.iota(io2, [[0, 1]], channel_multiplier=1,
                              allow_small_or_imprecise_dtypes=True), inc=SGPC, inc_k=1)
        P.ve(lambda e: e.memset(ones_bf[:], 1.0), inc=SDVE)
        P.ve(lambda e: e.memset(ones32[:], 1.0), inc=SDVE)
        P.ve(lambda e: e.memset(eps_c[:], EPS), inc=SDVE)
        P.ve(lambda e: e.tensor_scalar(ident[:], io1, io2, None, ALU.is_equal),
             wait=[SGPC], inc=SDVE)
        P.sy(lambda e: e.dma_start(out=b1_sb[:], in_=b1), inc=DX)
        P.gp(lambda e: e.dma_start(out=bgrow[:], in_=rbg), inc=DW)
        P.gp(lambda e: e.dma_start(
            out=rwg_sb[:].rearrange("p (c n) -> p c n", n=NEXP),
            in_=rwg.rearrange("(c p) n -> p c n", p=128)), inc=DW)
        barrier()

    def load_layer_params(l):
        for j, nm in enumerate(("g_q", "g_k", "g_v")):
            P.sy(lambda e, nm=nm, j=j: e.dma_start(
                out=gam[:, j * NCH:(j + 1) * NCH], in_=wq[(l, nm)]), inc=DX)
        for j, nm in enumerate(("b_q", "b_k", "b_v")):
            P.sy(lambda e, nm=nm, j=j: e.dma_start(
                out=betv[:, j * NCH:(j + 1) * NCH], in_=wq[(l, nm)]), inc=DX)
        P.sy(lambda e: e.dma_start(out=betarow[:], in_=wq[(l, "beta")]), inc=DX)
        barrier()
        P.ve(lambda e: e.tensor_copy(betv_bf[:], betv[:]), inc=SDVE)
        P.sc(lambda e: e.activation(betarow[:], betarow[:], AF.Exp), inc=SACT)
        P.sc(lambda e: e.activation(betarow[:], betarow[:], AF.Ln, bias=1.0), wait=[SACT], inc=SACT)
        P.ve(lambda e: e.tensor_scalar(betarow[:], betarow[:], 10.0, 0.1, ALU.min, ALU.max),
             wait=[SACT], inc=SDVE)
        P.ve(lambda e: e.tensor_scalar_mul(betarow[:], betarow[:], 0.125), inc=SDVE)
        P.te(lambda e: e.matmul(ps[:, 0:H], ones32[0:1, :], betarow[:],
                                start=True, stop=True), wait=[SDVE], inc=SPE)
        P.ve(lambda e: e.tensor_copy(beta8[:], ps[:, 0:H]), wait=[SPE], inc=SDVE)
        barrier()

    def emit_u(j, Wraw, brow_seg, ubrow, ubrow_bf, ub_rep):
        for jh in range(2):
            for c in range(NCH):
                P.te(lambda e, jh=jh, c=c, W=Wraw, j=j: e.matmul(
                    ps[0:1, 512 + jh * 512:512 + (jh + 1) * 512] if False else ps[0:1, jh * 512:(jh + 1) * 512],
                    betv_bf[:, j * NCH + c:j * NCH + c + 1],
                    W[:, c * D + jh * 512:c * D + (jh + 1) * 512],
                    start=(c == 0), stop=(c == NCH - 1)),
                    inc=SPE if c == NCH - 1 else None)
            P.ve(lambda e, jh=jh: e.tensor_copy(
                ubrow[:, jh * 512:(jh + 1) * 512], ps[0:1, jh * 512:(jh + 1) * 512]),
                wait=[SPE], inc=SDVE)
        P.ve(lambda e: e.tensor_add(ubrow[:], ubrow[:], brow_seg), wait=[dxv()], inc=SDVE)
        P.ve(lambda e: e.tensor_copy(ubrow_bf[:], ubrow[:]), inc=SDVE)
        for jh in range(2):
            P.te(lambda e, jh=jh: e.matmul(
                bank(2 + jh), ones_bf[0:1, :], ubrow_bf[:, jh * 512:(jh + 1) * 512],
                start=True, stop=True), wait=[SDVE], inc=SPE)
        P.ve(lambda e, j=j: e.tensor_copy(ub_rep[:, j * D:(j + 1) * D], bank(2, 2)),
             wait=[SPE], inc=SDVE)

    def emit_phaseA(l, s, xnT, sw, dbgA=False):
        xsrc = x_in if l == 0 else xmid
        kT, v_sb, ub_rep, betak = sw["kT"], sw["v_sb"], sw["ub_rep"], sw["betak"]

        A.push()
        kW = A.alloc("kW", [128, NCH * D], BF16)
        vW = A.alloc("vW", [128, NCH * D], BF16)
        xt = [A.alloc("xt", [128, D], F32) for _ in range(2)]
        xnsc = A.alloc("xnsc", [128, D], BF16)
        ksc = A.alloc("ksc", [128, D], F32)
        ksc_bf = A.alloc("ksc_bf", [128, D], BF16)
        brow = A.alloc("brow", [1, 2 * D], F32)
        ubrow = A.alloc("ubrow", [1, D], F32)
        ubrow_bf = A.alloc("ubrow_bf", [1, D], BF16)

        def wload(dst, src):
            P.gp(lambda e: e.dma_start(
                out=dst[:].rearrange("p (c n) -> p c n", n=D),
                in_=src.rearrange("(c p) n -> p c n", p=128)), inc=DW)
        wload(kW, wq[(l, "kwT")])
        wload(vW, wq[(l, "vwT")])
        wl_done = (DW, DW.n)
        for j, nm in enumerate(("kb", "vb")):
            P.sy(lambda e, nm=nm, j=j: e.dma_start(
                out=brow[:, j * D:(j + 1) * D], in_=wq[(l, nm)]), inc=DX)

        for tt in range(TT):
            buf = xt[tt % 2]
            P.sy(lambda e, b=buf, tt=tt, s=s: e.dma_start(
                out=b[:], in_=xsrc[s, tt * 128:(tt + 1) * 128, :]),
                wait=[SACT, SDVE] if tt >= 2 else [], inc=DX)
            P.ve(lambda e, b=buf, tt=tt: e.reduce_sum(sums[:, tt:tt + 1], b[:], axis=AX.X),
                 wait=[dxv()], inc=SDVE)
            P.sc(lambda e, b=buf, tt=tt: e.activation(
                trash[:], b[:], AF.Square, accum_out=sumsq[:, tt:tt + 1]),
                wait=[dxv()], inc=SACT)
        barrier()

        mean = st_tmp[:, 0:TT]
        var = st_tmp[:, TT:2 * TT]
        t3 = st_tmp[:, 2 * TT:3 * TT]
        P.ve(lambda e: e.tensor_scalar_mul(mean, sums[:], 1.0 / D), inc=SDVE)
        P.sc(lambda e: e.activation(var, sumsq[:], AF.Copy, scale=1.0 / D), inc=SACT)
        P.ve(lambda e: e.tensor_mul(t3, mean, mean), inc=SDVE)
        P.ve(lambda e: e.tensor_sub(var, var, t3), wait=[SACT], inc=SDVE)
        P.sc(lambda e: e.activation(t3, var, AF.Sqrt, bias=eps_c[:, 0:1]), wait=[SDVE], inc=SACT)
        P.ve(lambda e: e.reciprocal(st_scale[:], t3), wait=[SACT], inc=SDVE)
        P.ve(lambda e: e.scalar_tensor_tensor(st_bias[:], mean, -1.0, st_scale[:],
                                              ALU.mult, ALU.mult), inc=SDVE)
        P.ve(lambda e: e.tensor_mul(t3, st_scale[:], st_scale[:]), inc=SDVE)
        P.ve(lambda e: e.tensor_mul(t3, t3, var), inc=SDVE)
        P.sc(lambda e: e.activation(t3, t3, AF.Sqrt, bias=eps_c[:, 0:1]), wait=[SDVE], inc=SACT)
        P.ve(lambda e: e.reciprocal(st_r2[:], t3), wait=[SACT], inc=SDVE)
        barrier()
        if dbgA and l == 0 and s == 0:
            def _dumpst(e):
                return e.dma_start(out=dbg_st, in_=st_tmp[:, 0:0] if False else None)
            P.sy(lambda e: e.dma_start(out=dbg_st[:, 0:TT], in_=sums[:]), inc=DS)
            P.sy(lambda e: e.dma_start(out=dbg_st[:, TT:2 * TT], in_=sumsq[:]), inc=DS)
            P.sy(lambda e: e.dma_start(out=dbg_st[:, 2 * TT:3 * TT], in_=st_scale[:]), inc=DS)
            P.sy(lambda e: e.dma_start(out=dbg_st[:, 3 * TT:4 * TT], in_=st_bias[:]), inc=DS)
            P.sy(lambda e: e.dma_start(out=dbg_st[:, 4 * TT:5 * TT], in_=st_r2[:]), inc=DS)
            barrier()

        emit_u(1, kW, brow[:, 0:D], ubrow, ubrow_bf, ub_rep)
        emit_u(2, vW, brow[:, D:2 * D], ubrow, ubrow_bf, ub_rep)
        barrier()
        for j, Wt in ((1, kW), (2, vW)):
            for c in range(NCH):
                P.ve(lambda e, j=j, W=Wt, c=c: e.tensor_scalar_mul(
                    W[:, c * D:(c + 1) * D], W[:, c * D:(c + 1) * D],
                    gam[:, j * NCH + c:j * NCH + c + 1]), inc=SDVE)
        P.ve(lambda e: e.memset(
            v_sb[:].rearrange("p (t h c) -> p t h c", h=H, c=DH + 1)[:, :, :, DH:DH + 1],
            1.0), inc=SDVE)
        barrier()

        for half in range(2):
            for i in range(8):
                tt = half * 8 + i
                buf = xt[tt % 2]
                P.sy(lambda e, b=buf, tt=tt, s=s: e.dma_start(
                    out=b[:], in_=xsrc[s, tt * 128:(tt + 1) * 128, :]),
                    wait=[SDVE] if i >= 2 else [], inc=DX)
                P.ve(lambda e, b=buf, tt=tt: e.tensor_scalar(
                    xnsc[:], b[:], st_scale[:, tt:tt + 1], st_bias[:, tt:tt + 1],
                    ALU.mult, ALU.add),
                    wait=[dxv(), SPE], inc=SDVE)
                for c in range(NCH):
                    P.te(lambda e, c=c, i=i: e.transpose(
                        bank(c).bitcast(BF16)[:, i * 128:(i + 1) * 128],
                        xnsc[:, c * 128:(c + 1) * 128], ident[:]),
                        wait=[SDVE], inc=SPE if c == NCH - 1 else None)
            for c in range(NCH):
                P.ve(lambda e, c=c, half=half: e.tensor_copy(
                    xnT[:].rearrange("p (c t) -> p c t", t=T)[:, c, half * 1024:(half + 1) * 1024],
                    bank(c).bitcast(BF16)), wait=[SPE], inc=SDVE)
            barrier()

        for tt in range(TT):
            for c in range(NCH):
                for (W, bk) in ((kW, 0), (vW, 2)):
                    for jh in range(2):
                        P.te(lambda e, c=c, tt=tt, W=W, bk=bk, jh=jh: e.matmul(
                            bank(bk + jh),
                            xnT[:, c * T + tt * 128:c * T + (tt + 1) * 128],
                            W[:, c * D + jh * 512:c * D + (jh + 1) * 512],
                            start=(c == 0), stop=(c == NCH - 1)),
                            wait=[SDVE] if (c == 0 and bk == 0 and jh == 0) else [],
                            inc=SPE if (c == NCH - 1 and bk == 2 and jh == 1) else None)
            for jh in range(2):
                P.ve(lambda e, jh=jh, tt=tt: e.scalar_tensor_tensor(
                    ksc[:, jh * 512:(jh + 1) * 512], bank(jh),
                    st_r2[:, tt:tt + 1],
                    ub_rep[:, D + jh * 512:D + (jh + 1) * 512],
                    ALU.mult, ALU.add), wait=[SPE], inc=SDVE)
                P.ve(lambda e, jh=jh, tt=tt: e.scalar_tensor_tensor(
                    v_sb[:].rearrange("p (t h c) -> p t h c", h=H, c=DH + 1)
                    [:, tt, jh * 8:(jh + 1) * 8, 0:DH],
                    bank(2 + jh).rearrange("p (h c) -> p h c", c=DH),
                    st_r2[:, tt:tt + 1],
                    ub_rep[:, 2 * D + jh * 512:2 * D + (jh + 1) * 512].rearrange(
                        "o (h c) -> o h c", c=DH),
                    ALU.mult, ALU.add), inc=SDVE)
            P.sc(lambda e, tt=tt: e.activation(trash[:], ksc[:], AF.Square,
                                               accum_out=rk_rec[:, tt:tt + 1]),
                 wait=[SDVE], inc=SACT)
            P.sc(lambda e, tt=tt: e.activation(rk_rec[:, tt:tt + 1], rk_rec[:, tt:tt + 1],
                                               AF.Sqrt), wait=[SACT], inc=SACT)
            P.ve(lambda e, tt=tt: e.tensor_scalar_max(rk_rec[:, tt:tt + 1], rk_rec[:, tt:tt + 1],
                                                      EPSN), wait=[SACT], inc=SDVE)
            P.ve(lambda e, tt=tt: e.reciprocal(rk_rec[:, tt:tt + 1], rk_rec[:, tt:tt + 1]),
                 inc=SDVE)
            P.ve(lambda e, tt=tt: e.tensor_scalar_mul(ksc_bf[:], ksc[:], rk_rec[:, tt:tt + 1]),
                 inc=SDVE)
            for c in range(NCH):
                P.te(lambda e, c=c, tt=tt: e.transpose(
                    bank(4 + tt % 2).bitcast(BF16)[:, c * 128:(c + 1) * 128],
                    ksc_bf[:, c * 128:(c + 1) * 128], ident[:]),
                    wait=[SDVE], inc=SPE if c == NCH - 1 else None)
            P.ve(lambda e, tt=tt: e.tensor_copy(
                kT[:].rearrange("p (c t) -> p c t", t=T)[:, :, tt * 128:(tt + 1) * 128],
                bank(4 + tt % 2).bitcast(BF16).rearrange("p (c t) -> p c t", t=128)),
                wait=[SPE], inc=SDVE)
        barrier()

        if l == 1:
            pcol = st_tmp[:, 3 * TT:3 * TT + NEXP]
            mcol = st_tmp[:, 3 * TT + NEXP:3 * TT + 2 * NEXP]
            for tt in range(TT):
                for c in range(NCH):
                    P.te(lambda e, c=c, tt=tt: e.matmul(
                        ps[:, 0:NEXP],
                        xnT[:, c * T + tt * 128:c * T + (tt + 1) * 128],
                        rwg_sb[:, c * NEXP:(c + 1) * NEXP],
                        start=(c == 0), stop=False),
                        wait=[SACT] if c == 0 else [], inc=None)
                P.te(lambda e: e.matmul(ps[:, 0:NEXP], ones_bf[0:1, :], bgrow[:],
                                        start=False, stop=True), inc=SPE)
                P.sc(lambda e, pcol=pcol: e.activation(pcol, ps[:, 0:NEXP], AF.Sigmoid),
                     wait=[SPE], inc=SACT)
                P.ve(lambda e, pcol=pcol, mcol=mcol: e.tensor_scalar(
                    mcol, pcol, 0.5, None, ALU.is_gt), wait=[SACT], inc=SDVE)
                P.ve(lambda e, s=s, tt=tt, pcol=pcol, mcol=mcol: e.tensor_tensor(
                    w_moe[:, (s * TT + tt) * NEXP:(s * TT + tt + 1) * NEXP],
                    pcol, mcol, ALU.mult), inc=SDVE)
        barrier()
        A.pop()

    def emit_attn(l, s, xnT, sw, dbg=None):
        xsrc = x_in if l == 0 else xmid
        kT, v_sb, ub_rep, betak = sw["kT"], sw["v_sb"], sw["ub_rep"], sw["betak"]
        A.push()
        qW = A.alloc("qW", [128, NCH * D], BF16)
        oW = A.alloc("oW", [64, H * D], BF16)
        qTb = A.alloc("qTb", [128, NCH * QB], BF16)
        E_sb = A.alloc("E_sb", [128, 4096], BF16)
        attnT = A.alloc("attnT", [128, H * QB], BF16)
        rs_rec = A.alloc("rs_rec", [128, 512], F32)
        obrow = A.alloc("obrow", [1, D], BF16)

        A.push()
        brow_q = A.alloc("brow_q", [1, D], F32)
        ubrow = A.alloc("ubrow_a", [1, D], F32)
        ubrow_bf = A.alloc("ubrow_abf", [1, D], BF16)

        P.gp(lambda e: e.dma_start(
            out=qW[:].rearrange("p (c n) -> p c n", n=D),
            in_=wq[(l, "qwT")].rearrange("(c p) n -> p c n", p=128)), inc=DW)
        P.gp(lambda e: e.dma_start(
            out=oW[:].rearrange("p (h n) -> p h n", n=D),
            in_=wq[(l, "owT")].rearrange("(h p) n -> p h n", p=64)), inc=DW)
        P.gp(lambda e: e.dma_start(out=obrow[:], in_=wq[(l, "ob")]), inc=DW)
        P.sy(lambda e: e.dma_start(out=brow_q[:], in_=wq[(l, "qb")]), inc=DX)
        P.ve(lambda e: e.memset(rs_rec[:], 0.0), inc=SDVE)
        barrier()
        emit_u(0, qW, brow_q[:], ubrow, ubrow_bf, ub_rep)
        barrier()
        for c in range(NCH):
            P.ve(lambda e, c=c: e.tensor_scalar_mul(
                qW[:, c * D:(c + 1) * D], qW[:, c * D:(c + 1) * D],
                gam[:, c:c + 1]), inc=SDVE)
        barrier()
        if dbg is not None and l == 0 and s == 0:
            P.sy(lambda e, ub=ub_rep: e.dma_start(out=dbg_ub, in_=ub[:]), inc=DS)
            barrier()
        A.pop()

        if dbg is not None:
            P.ve(lambda e: e.memset(qTb[:], 0.0), inc=SDVE)
            P.ve(lambda e: e.memset(attnT[:], 0.0), inc=SDVE)
            barrier()
        qsc = A.alloc("qsc", [128, D], F32)
        qsc_bf = A.alloc("qsc_bf", [128, D], BF16)
        rqss = A.alloc("rqss", [128, 4], F32)
        xe = [A.alloc("xe", [128, D], F32)]
        xne = None
        stile = None

        pv_bank_free = [0, 0]
        bc_mark = [0]
        for qb in range(NQB):
            for i in range(4):
                tt = qb * 4 + i
                for c in range(NCH):
                    for jh in range(2):
                        P.te(lambda e, c=c, tt=tt, jh=jh: e.matmul(
                            bank(jh), xnT[:, c * T + tt * 128:c * T + (tt + 1) * 128],
                            qW[:, c * D + jh * 512:c * D + (jh + 1) * 512],
                            start=(c == 0), stop=(c == NCH - 1)),
                            wait=[SDVE] if (c == 0 and jh == 0) else [],
                            inc=SPE if (c == NCH - 1 and jh == 1) else None)
                for jh in range(2):
                    P.ve(lambda e, jh=jh, tt=tt: e.scalar_tensor_tensor(
                        qsc[:, jh * 512:(jh + 1) * 512], bank(jh),
                        st_r2[:, tt:tt + 1], ub_rep[:, jh * 512:(jh + 1) * 512],
                        ALU.mult, ALU.add), wait=[SPE], inc=SDVE)
                P.sc(lambda e, i=i: e.activation(trash[:], qsc[:], AF.Square,
                                                 accum_out=rqss[:, i:i + 1]),
                     wait=[SDVE], inc=SACT)
                P.sc(lambda e, i=i: e.activation(rqss[:, i:i + 1], rqss[:, i:i + 1],
                                                 AF.Sqrt), wait=[SACT], inc=SACT)
                P.ve(lambda e, i=i: e.tensor_scalar_max(rqss[:, i:i + 1], rqss[:, i:i + 1],
                                                        EPSN), wait=[SACT], inc=SDVE)
                P.ve(lambda e, i=i: e.reciprocal(rqss[:, i:i + 1], rqss[:, i:i + 1]),
                     inc=SDVE)
                P.ve(lambda e, i=i: e.tensor_scalar_mul(qsc_bf[:], qsc[:], rqss[:, i:i + 1]),
                     inc=SDVE)
                if dbg is not None and l == 0 and s == 0 and qb == 0 and i == 0:
                    P.sy(lambda e: e.dma_start(out=dbg_qsc, in_=qsc[:]), wait=[SDVE], inc=DS)
                    P.sy(lambda e: e.dma_start(out=dbg_rq[:, 0:1], in_=rqss[:, 0:1], allow_slow_non_contiguous=True), wait=[SDVE], inc=DS)
                    barrier()
                for c in range(NCH):
                    P.te(lambda e, c=c, i=i: e.transpose(
                        bank(2 + i % 2).bitcast(BF16)[:, c * 128:(c + 1) * 128],
                        qsc_bf[:, c * 128:(c + 1) * 128], ident[:]),
                        wait=[SDVE], inc=SPE if c == NCH - 1 else None)
                P.ve(lambda e, i=i: e.tensor_copy(
                    qTb[:].rearrange("p (c t) -> p c t", t=QB)[:, :, i * 128:(i + 1) * 128],
                    bank(2 + i % 2).bitcast(BF16).rearrange("p (c t) -> p c t", t=128)),
                    wait=[SPE], inc=SDVE)
            barrier()
            if dbg is not None and qb == 0:
                P.sy(lambda e, s=s: e.dma_start(out=dbg[0][s], in_=qTb[:]), inc=DS)
                barrier()

            for pr in range(H // 2):
                c = pr
                s_marks = [[], []]
                exp_marks = [[], []]
                pv_marks = [[], []]

                def emit_S_k2(hh, ktg, k2):
                    rb = hh * 64
                    sbp = hh * 2
                    kt = 2 * ktg + k2
                    wl = []
                    if ktg >= 1 and k2 == 0:
                        wl.append((SACT, exp_marks[hh][ktg - 1]))
                    P.te(lambda e, c=c, rb=rb, sbp=sbp, kt=kt, k2=k2: e.matmul(
                        ps[:, (sbp + k2) * 512:(sbp + k2 + 1) * 512],
                        kT[rb:rb + 64, c * T + kt * 128:c * T + (kt + 1) * 128],
                        qTb[rb:rb + 64, c * QB:(c + 1) * QB],
                        start=True, stop=True, tile_position=(rb, 0)),
                        wait=wl, inc=SPE)

                def emit_exp(hh, ktg):
                    h = 2 * pr + hh
                    sbp = hh * 2
                    wl = [(SPE, s_marks[hh][ktg])]
                    if ktg >= 2:
                        wl.append((SPE, pv_marks[hh][ktg - 2]))
                    P.sc(lambda e, sbp=sbp, h=h, hh=hh, ktg=ktg: e.activation(
                        E_sb[:, hh * 2048 + (ktg % 2) * 1024:hh * 2048 + (ktg % 2) * 1024 + 1024],
                        ps[:, sbp * 512:(sbp + 2) * 512], AF.Exp,
                        scale=beta8[:, h:h + 1]),
                        wait=wl)
                    exp_marks[hh].append(SACT.n)

                def emit_PV(hh, ktg):
                    h = 2 * pr + hh
                    for k2 in range(2):
                        kt = 2 * ktg + k2
                        wl = [(SACT, exp_marks[hh][ktg])]
                        if kt == 0:
                            wl.append((SDVE, pv_bank_free[hh]))
                        P.te(lambda e, kt=kt, h=h, hh=hh, ktg=ktg, k2=k2: e.matmul(
                            ps[0:65, (4 + hh) * 512:(5 + hh) * 512],
                            v_sb[:].rearrange("p (t h c) -> p t h c", h=H, c=DH + 1)[:, kt, h, :],
                            E_sb[:, hh * 2048 + (ktg % 2) * 1024 + k2 * 512:
                                 hh * 2048 + (ktg % 2) * 1024 + (k2 + 1) * 512],
                            start=(kt == 0), stop=(kt == TT - 1)),
                            wait=wl, inc=SPE if k2 == 1 else None)
                    pv_marks[hh].append(SPE.n)

                for ktg in range(8):
                    emit_S_k2(0, ktg, 0)
                    emit_S_k2(1, ktg, 0)
                    emit_S_k2(0, ktg, 1)
                    emit_S_k2(1, ktg, 1)
                    for hh in range(2):
                        s_marks[hh].append(SPE.n)
                    for hh in range(2):
                        emit_exp(hh, ktg)
                    if ktg >= 1:
                        for hh in range(2):
                            emit_PV(hh, ktg - 1)
                for hh in range(2):
                    emit_PV(hh, 7)
                for hh in range(2):
                    h = 2 * pr + hh
                    P.ve(lambda e, h=h, hh=hh: e.tensor_copy(
                        attnT[0:64, h * QB:(h + 1) * QB],
                        ps[0:64, (4 + hh) * 512:(5 + hh) * 512]), wait=[SPE], inc=SDVE)
                    P.ve(lambda e, hh=hh: e.reciprocal(
                        rs_rec[64:65, :], ps[64:65, (4 + hh) * 512:(5 + hh) * 512]),
                        wait=[(SPE, bc_mark[0])], inc=SDVE)
                    pv_bank_free[hh] = SDVE.n
                    P.te(lambda e: e.matmul(bank(6), ones32[64:65, :], rs_rec[64:65, :],
                                            start=True, stop=True), wait=[SDVE], inc=SPE)
                    bc_mark[0] = SPE.n
                    P.ve(lambda e, h=h: e.tensor_tensor(
                        attnT[0:64, h * QB:(h + 1) * QB],
                        attnT[0:64, h * QB:(h + 1) * QB], bank(6)[0:64, :], ALU.mult),
                        wait=[(SPE, bc_mark[0])], inc=SDVE)
            barrier()
            if dbg is not None and qb == 0:
                P.sy(lambda e, s=s: e.dma_start(out=dbg[1][s], in_=attnT[:]), inc=DS)
                barrier()

            for i in range(4):
                tt = qb * 4 + i
                P.sy(lambda e, i=i, tt=tt, s=s: e.dma_start(
                    out=xe[0][:], in_=xsrc[s, tt * 128:(tt + 1) * 128, :]),
                    wait=[SDVE] if i >= 1 else [], inc=DX)
                for jh in range(2):
                    bk = i * 2 + jh
                    for h in range(H):
                        P.te(lambda e, h=h, i=i, jh=jh, bk=bk: e.matmul(
                            bank(bk), attnT[0:64, h * QB + i * 128:h * QB + (i + 1) * 128],
                            oW[0:64, h * D + jh * 512:h * D + (jh + 1) * 512],
                            start=(h == 0), stop=False), inc=None)
                    P.te(lambda e, jh=jh, bk=bk: e.matmul(
                        bank(bk), ones_bf[0:1, :], obrow[:, jh * 512:(jh + 1) * 512],
                        start=False, stop=True), inc=SPE)
                P.ve(lambda e, i=i, tt=tt: e.tensor_scalar(
                    qsc[:].bitcast(BF16)[:, 0:D], xe[0][:], st_scale[:, tt:tt + 1], st_bias[:, tt:tt + 1],
                    ALU.mult, ALU.add), wait=[dxv(), SPE], inc=SDVE)
                for jh in range(2):
                    P.ve(lambda e, i=i, jh=jh: e.tensor_add(
                        qsc[:].bitcast(BF16)[:, D + jh * 512:D + (jh + 1) * 512],
                        bank(i * 2 + jh),
                        qsc[:].bitcast(BF16)[:, jh * 512:jh * 512 + 512]),
                        wait=[(DS, DS.n)] if jh == 0 else [], inc=SDVE)
                P.sy(lambda e, tt=tt, s=s: e.dma_start(
                    out=ssp[s, tt * 128:(tt + 1) * 128, :],
                    in_=qsc[:].bitcast(BF16)[:, D:2 * D]),
                    wait=[SDVE], inc=DS)
            barrier()
        if l == 1:
            P.sy(lambda e, s=s: e.dma_start(out=xnsp[s], in_=xnT[:]), inc=DS)
            barrier()
        A.pop()

    def emit_ffn0(s, xnT, dbg=False):
        A.push()
        wgS = A.alloc("wgS", [128, NCH * L0H], BF16)
        wuS = A.alloc("wuS", [128, NCH * L0H], BF16)
        wdS = A.alloc("wdS", [128, NH0 * D], BF16)
        sgb = [A.alloc("sgb", [128, 256], BF16) for _ in range(2)]
        hb = [A.alloc("hb", [128, 256], BF16) for _ in range(2)]
        gxe = A.alloc("gxe", [128, D], F32)
        gse = A.alloc("gse", [128, D], BF16)
        ggb = A.alloc("ggb", [128, D], BF16)
        gob = A.alloc("gob", [128, D], F32)

        P.gp(lambda e: e.dma_start(
            out=wgS[:].rearrange("p (c n) -> p c n", n=L0H),
            in_=wg_t.rearrange("(c p) n -> p c n", p=128)), inc=DW)
        P.gp(lambda e: e.dma_start(
            out=wuS[:].rearrange("p (c n) -> p c n", n=L0H),
            in_=wu_t.rearrange("(c p) n -> p c n", p=128)), inc=DW)
        P.gp(lambda e: e.dma_start(
            out=wdS[:].rearrange("p (c n) -> p c n", n=D),
            in_=wd_t.rearrange("(c p) n -> p c n", p=128)), inc=DW)
        barrier()

        for tq in range(T // 256):
            stt_mark = {}
            silu_mark = {}
            down_mark = {}

            def emit_gu(ht):
                gb, ub2 = ht % 2, 2 + ht % 2
                for c in range(NCH):
                    P.te(lambda e, c=c, ht=ht, tq=tq, gb=gb: e.matmul(
                        ps[:, gb * 512:gb * 512 + 256],
                        wgS[:, c * L0H + ht * 128:c * L0H + (ht + 1) * 128],
                        xnT[:].rearrange("p (c t) -> p c t", t=T)[:, c, tq * 256:(tq + 1) * 256],
                        start=(c == 0), stop=(c == NCH - 1)),
                        wait=[(SACT, silu_mark[ht - 2])] if (ht >= 2 and c == 0) else [],
                        inc=None)
                for c in range(NCH):
                    P.te(lambda e, c=c, ht=ht, tq=tq, ub2=ub2: e.matmul(
                        ps[:, ub2 * 512:ub2 * 512 + 256],
                        wuS[:, c * L0H + ht * 128:c * L0H + (ht + 1) * 128],
                        xnT[:].rearrange("p (c t) -> p c t", t=T)[:, c, tq * 256:(tq + 1) * 256],
                        start=(c == 0), stop=(c == NCH - 1)),
                        wait=[(SDVE, stt_mark[ht - 2])] if (ht >= 2 and c == 0) else [],
                        inc=SPE if c == NCH - 1 else None)
                P.sc(lambda e, gb=gb: e.activation(
                    sgb[ht % 2][:], ps[:, gb * 512:gb * 512 + 256], AF.Silu, scale=1.0 / 127.0),
                    wait=[SPE, (SDVE, stt_mark.get(ht - 2, 0))], self_wait=False)
                silu_mark[ht] = SACT.n
                P.ve(lambda e, ub2=ub2, ht=ht: e.scalar_tensor_tensor(
                    hb[ht % 2][:], ps[:, ub2 * 512:ub2 * 512 + 256], 1.0 / 127.0,
                    sgb[ht % 2][:], ALU.mult, ALU.mult),
                    wait=[(SACT, silu_mark[ht]), (SPE, down_mark.get(ht - 2, 0))], inc=SDVE)
                stt_mark[ht] = SDVE.n

            def emit_down(ht):
                for tsub in range(2):
                    for jh in range(2):
                        P.te(lambda e, ht=ht, tsub=tsub, jh=jh: e.matmul(
                            bank(4 + tsub * 2 + jh),
                            hb[ht % 2][:, tsub * 128:(tsub + 1) * 128],
                            wdS[:, ht * D + jh * 512:ht * D + (jh + 1) * 512],
                            start=(ht == 0), stop=(ht == NH0 - 1)),
                            wait=[(SDVE, stt_mark[ht])] if (tsub == 0 and jh == 0) else [],
                            inc=SPE if (tsub == 1 and jh == 1) else None)
                down_mark[ht] = SPE.n

            for ht in range(NH0):
                emit_gu(ht)
                if ht >= 1:
                    emit_down(ht - 1)
            emit_down(NH0 - 1)
            for tsub in range(2):
                tt = tq * 2 + tsub
                P.sy(lambda e, s=s, tt=tt: e.dma_start(
                    out=gxe[:], in_=x_in[s, tt * 128:(tt + 1) * 128, :]),
                    wait=[SDVE] if tsub == 1 else [], inc=DX)
                P.sy(lambda e, s=s, tt=tt: e.dma_start(
                    out=gse[:], in_=ssp[s, tt * 128:(tt + 1) * 128, :]), inc=DX)
                for jh in range(2):
                    P.sc(lambda e, tsub=tsub, jh=jh: e.activation(
                        ggb[:, jh * 512:(jh + 1) * 512], bank(4 + tsub * 2 + jh),
                        AF.Silu, scale=1.0 / 127.0), wait=[SPE, SDVE], inc=SACT)
                if dbg:
                    P.sy(lambda e, s=s, tt=tt: e.dma_start(
                        out=dbg_g[s, tt * 128:(tt + 1) * 128, :], in_=ggb[:]),
                        wait=[SACT], inc=DS)
                    barrier()
                P.ve(lambda e: e.tensor_mul(ggb[:], gse[:], ggb[:]),
                     wait=[SACT, dxv()], inc=SDVE)
                P.ve(lambda e: e.tensor_add(gob[:], gxe[:], ggb[:]),
                     wait=[(DS, DS.n)], inc=SDVE)
                P.sy(lambda e, s=s, tt=tt: e.dma_start(
                    out=xmid[s, tt * 128:(tt + 1) * 128, :], in_=gob[:]),
                    wait=[SDVE], inc=DS)
            barrier()
        A.pop()

    def emit_ffn1():
        A.push()
        w1S = A.alloc("w1S", [128, NCH * L1H], BF16)
        w2S = A.alloc("w2S", [128, NH1 * D], BF16)
        acc = A.alloc("acc", [128, 8 * D], F32)
        xnh = A.alloc("xnh", [128, NCH * 1024], BF16)
        b2e = A.alloc("b2e", [1, D], BF16)
        hb = [A.alloc("hbm", [128, 256], BF16) for _ in range(2)]
        gxe = A.alloc("gxe1", [128, D], F32)
        gse = A.alloc("gse1", [128, D], BF16)
        ggb = A.alloc("ggb1", [128, D], BF16)
        gob = A.alloc("gob1", [128, D], F32)

        w1c_reader = [0] * NCH
        w2g_reader = [0] * 8
        b2_reader = [0]
        for sw_i in range(4):
            s, half = sw_i // 2, sw_i % 2
            P.sy(lambda e, s=s, half=half: e.dma_start(
                out=xnh[:].rearrange("p (c t) -> p c t", t=1024),
                in_=xnsp[s].rearrange("p (c t) -> p c t", t=T)[:, :, half * 1024:(half + 1) * 1024]),
                inc=DX)
            barrier()
            acc_mark = [0] * 8
            last_acc = [0]
            for e_ in range(NEXP):
                w1c_mark = []
                for cc in range(NCH):
                    P.gp(lambda e, e_=e_, cc=cc: e.dma_start(
                        out=w1S[:, cc * L1H:(cc + 1) * L1H],
                        in_=w1[e_, cc * 128:(cc + 1) * 128, :]),
                        wait=[(SPE, w1c_reader[cc])], inc=DW)
                    w1c_mark.append(DW.n)
                w2g_mark = []
                for gg in range(8):
                    P.gp(lambda e, e_=e_, gg=gg: e.dma_start(
                        out=w2S[:, gg * 4096:(gg + 1) * 4096],
                        in_=w2[e_, gg * 512:(gg + 1) * 512, :]),
                        wait=[(SPE, w2g_reader[gg])], inc=DW)
                    w2g_mark.append(DW.n)
                P.gp(lambda e, e_=e_: e.dma_start(
                    out=b2e[:], in_=b2[0:1, e_ * D:(e_ + 1) * D]),
                    wait=[(SPE, b2_reader[0])], inc=DW)
                b2_mark = DW.n
                for tq in range(4):
                    relu_mark = {}

                    def emit_w1(ht):
                        pb = ht % 2
                        for c in range(NCH):
                            wl = []
                            if c == 0 and ht >= 2:
                                wl.append((SACT, relu_mark[ht - 2]))
                            if tq == 0:
                                wl.append((DW, w1c_mark[c]))
                            P.te(lambda e, c=c, ht=ht, tq=tq, pb=pb: e.matmul(
                                ps[:, pb * 512:pb * 512 + 256],
                                w1S[:, c * L1H + ht * 128:c * L1H + (ht + 1) * 128],
                                xnh[:, c * 1024 + tq * 256:c * 1024 + (tq + 1) * 256],
                                start=(c == 0), stop=(c == NCH - 1)),
                                wait=wl, inc=SPE if c == NCH - 1 else None)
                        P.sc(lambda e, pb=pb, ht=ht, e_=e_: e.activation(
                            hb[ht % 2][:], ps[:, pb * 512:pb * 512 + 256], AF.Relu,
                            bias=b1_sb[:, e_ * NH1 + ht:e_ * NH1 + ht + 1]),
                            wait=[SPE], self_wait=False)
                        relu_mark[ht] = SACT.n

                    def emit_w2(ht):
                        for tsub in range(2):
                            for jh in range(2):
                                wl = [(SACT, relu_mark[ht])] if (tsub == 0 and jh == 0) else []
                                if ht == 0 and tsub == 0 and jh == 0:
                                    wl.append((SDVE, last_acc[0]))
                                if tq == 0 and ht % 4 == 0 and tsub == 0 and jh == 0:
                                    wl.append((DW, w2g_mark[ht // 4]))
                                P.te(lambda e, ht=ht, tsub=tsub, jh=jh: e.matmul(
                                    bank(4 + tsub * 2 + jh),
                                    hb[ht % 2][:, tsub * 128:(tsub + 1) * 128],
                                    w2S[:, ht * D + jh * 512:ht * D + (jh + 1) * 512],
                                    start=(ht == 0), stop=False),
                                    wait=wl, inc=None)

                    for ht in range(NH1):
                        emit_w1(ht)
                        if ht >= 1:
                            emit_w2(ht - 1)
                    emit_w2(NH1 - 1)
                    for c in range(NCH):
                        w1c_reader[c] = SPE.n
                    for tsub in range(2):
                        for jh in range(2):
                            P.te(lambda e, tsub=tsub, jh=jh, e_=e_: e.matmul(
                                bank(4 + tsub * 2 + jh), ones_bf[0:1, :],
                                b2e[:, jh * 512:(jh + 1) * 512],
                                start=False, stop=True),
                                wait=[(DW, b2_mark)] if (tsub == 0 and jh == 0) else [],
                                inc=SPE if (tsub == 1 and jh == 1) else None)
                    if tq == 3:
                        b2_reader[0] = SPE.n
                        for gg in range(8):
                            w2g_reader[gg] = SPE.n
                    for tsub in range(2):
                        gt = tq * 2 + tsub
                        wcol = w_moe[:, (s * TT + half * 8 + gt) * NEXP + e_:
                                     (s * TT + half * 8 + gt) * NEXP + e_ + 1]
                        at = acc[:, gt * D:(gt + 1) * D]
                        for jh in range(2):
                            src2 = bank(4 + tsub * 2 + jh)
                            dst = at[:, jh * 512:(jh + 1) * 512]
                            if e_ == 0:
                                P.ve(lambda e, dst=dst, src=src2, wcol=wcol: e.tensor_scalar_mul(
                                    dst, src, wcol), wait=[SPE], inc=SDVE)
                            else:
                                P.ve(lambda e, dst=dst, src=src2, wcol=wcol: e.scalar_tensor_tensor(
                                    dst, src, wcol, dst, ALU.mult, ALU.add),
                                    wait=[SPE], inc=SDVE)
                    acc_mark[tq * 2] = SDVE.n
                    acc_mark[tq * 2 + 1] = SDVE.n
                    last_acc[0] = SDVE.n
            barrier()
            for i in range(8):
                tt = half * 8 + i
                P.sy(lambda e, s=s, tt=tt: e.dma_start(
                    out=gxe[:], in_=xmid[s, tt * 128:(tt + 1) * 128, :]),
                    wait=[SDVE] if i >= 1 else [], inc=DX)
                P.sy(lambda e, s=s, tt=tt: e.dma_start(
                    out=gse[:], in_=ssp[s, tt * 128:(tt + 1) * 128, :]), inc=DX)
                P.sc(lambda e, i=i: e.activation(ggb[:], acc[:, i * D:(i + 1) * D], AF.Silu),
                     wait=[dxv(), SDVE])
                P.ve(lambda e: e.tensor_mul(ggb[:], gse[:], ggb[:]), wait=[SACT], inc=SDVE)
                P.ve(lambda e: e.tensor_add(gob[:], gxe[:], ggb[:]),
                     wait=[(DS, DS.n)], inc=SDVE)
                P.sy(lambda e, s=s, tt=tt: e.dma_start(
                    out=y_out[s, tt * 128:(tt + 1) * 128, :], in_=gob[:]),
                    wait=[SDVE], inc=DS)
            barrier()
        A.pop()

    emit_setup()
    for l in (0, 1):
        load_layer_params(l)
        for s in range(B):
            A.push()
            xnT = A.alloc("xnT", [128, NCH * T], BF16)
            A.push()
            sw = {
                "kT": A.alloc("kT", [128, NCH * T], BF16),
                "v_sb": A.alloc("v_sb", [128, TT * H * (DH + 1)], BF16),
                "ub_rep": A.alloc("ub_rep", [128, 3 * D], BF16),
                "betak": A.alloc("betak", [128, TT * H], F32),
            }
            emit_phaseA(l, s, xnT, sw, dbgA=debug)
            if debug and l == 0:
                P.sy(lambda e, s=s, xnT=xnT: e.dma_start(out=dbg_xnt0[s], in_=xnT[:]), inc=DS)
                P.sy(lambda e, s=s, kt=sw["kT"]: e.dma_start(out=dbg_kt0[s], in_=kt[:]), inc=DS)
                P.sy(lambda e, s=s, vv=sw["v_sb"]: e.dma_start(out=dbg_v0[s], in_=vv[:]), inc=DS)
                barrier()
            emit_attn(l, s, xnT, sw, dbg=(dbg_qt0, dbg_at0) if (debug and l == 0) else None)
            A.pop()
            if l == 0:
                emit_ffn0(s, xnT, dbg=debug)
                if debug:
                    P.sy(lambda e, s=s: e.dma_start(out=dbg_s0[s], in_=ssp[s]), inc=DS)
                    barrier()
            A.pop()
        if l == 1:
            emit_ffn1()

    if debug:
        P.sy(lambda e: e.dma_start(out=dbg_xmid, in_=xmid), inc=DS)
    P.sy(lambda e: e.nop(), wait=[(DS, DS.n), (DX, DX.n), (DW, DW.n)])
    P.play()
    return nc


def _prep_inputs(inputs):
    f = np.ascontiguousarray

    def pm(vec):
        return f(vec.reshape(-1, 128).T)

    m = {}
    for l in (0, 1):
        p = f"L{l}_attn_"
        m[f"l{l}_qwT"] = f(inputs[p + "q_w"].T)
        m[f"l{l}_kwT"] = f(inputs[p + "k_w"].T)
        m[f"l{l}_vwT"] = f(inputs[p + "v_w"].T)
        m[f"l{l}_owT"] = f(inputs[p + "o_w"].T)
        m[f"l{l}_g_q"] = pm(inputs[p + "ln_q_g"])
        m[f"l{l}_g_k"] = pm(inputs[p + "ln_k_g"])
        m[f"l{l}_g_v"] = pm(inputs[p + "ln_v_g"])
        m[f"l{l}_b_q"] = pm(inputs[p + "ln_q_b"])
        m[f"l{l}_b_k"] = pm(inputs[p + "ln_k_b"])
        m[f"l{l}_b_v"] = pm(inputs[p + "ln_v_b"])
        m[f"l{l}_qb"] = f(inputs[p + "q_b"].reshape(1, -1))
        m[f"l{l}_kb"] = f(inputs[p + "k_b"].reshape(1, -1))
        m[f"l{l}_vb"] = f(inputs[p + "v_b"].reshape(1, -1))
        m[f"l{l}_ob"] = f(inputs[p + "o_b"].reshape(1, -1))
        m[f"l{l}_beta"] = f(inputs[p + "beta_param"].reshape(1, -1))
    m["wg_t"] = f(inputs["L0_ffn_w_gate"].T)
    m["wu_t"] = f(inputs["L0_ffn_w_up"].T)
    m["wd_t"] = f(inputs["L0_ffn_w_down"].T)
    m["rwg"] = f(inputs["L1_ffn_Wg"])
    m["rbg"] = f(inputs["L1_ffn_bg"].reshape(1, -1))
    m["w1"] = f(inputs["L1_ffn_W1"])
    b1v = inputs["L1_ffn_b1"]
    m["b1"] = f(b1v.reshape(NEXP, NH1, 128).transpose(2, 0, 1).reshape(128, -1))
    m["w2"] = f(inputs["L1_ffn_W2"])
    m["b2"] = f(inputs["L1_ffn_b2"].reshape(1, -1))
    return m


def kernel(**inputs):
    x = np.ascontiguousarray(inputs["L0_attn_query"])
    shared = _prep_inputs(inputs)
    nc = build()
    in_maps = []
    for c in range(8):
        im = dict(shared)
        im["x"] = np.ascontiguousarray(x[c * B:(c + 1) * B])
        in_maps.append(im)
    res = run_bass_kernel_spmd(nc, in_maps, core_ids=list(range(8)))
    return np.concatenate([r["y"] for r in res.results], axis=0)


# version 08
# speedup vs baseline: 1.0029x; 1.0029x over previous
import numpy as np
import ml_dtypes
import concourse.bass as bass
import concourse.mybir as mybir
from concourse.bass_utils import run_bass_kernel_spmd

F32 = mybir.dt.float32
BF16 = mybir.dt.bfloat16
I32 = mybir.dt.int32
U8 = mybir.dt.uint8
AF = mybir.ActivationFunctionType
ALU = mybir.AluOpType
AX = mybir.AxisListType

B = 2
T = 2048
D = 1024
H = 16
DH = 64
TT = T // 128
NCH = D // 128
QB = 512
NQB = T // QB
L0H = 3072
NH0 = L0H // 128
L1H = 4096
NH1 = L1H // 128
NEXP = 8
EPS = 1e-5
EPSN = 1e-6

POOL_BYTES = 210944


class Sem:
    def __init__(self, h):
        self.h = h
        self.n = 0


class Prog:

    def __init__(self, nc):
        self.nc = nc
        self.q = {e: [] for e in ("tensor", "scalar", "vector", "sync", "gpsimd")}
        self.self_sems = {}

    def _push(self, eng, fn, wait, inc, inc_k, self_wait=True):
        waits = []
        ss = self.self_sems.get(eng)
        if ss is not None and self_wait:
            waits.append((ss.h, ss.n))
        for w in wait:
            if isinstance(w, tuple):
                waits.append((w[0].h, w[1]))
            else:
                waits.append((w.h, w.n))
        if inc is None and ss is not None:
            inc = ss
            inc_k = 1
        if inc is False:
            inc = None
        if inc is not None:
            inc.n += inc_k
            incs = (inc.h, inc_k)
        else:
            incs = None

        def play(e):
            for h, v in waits:
                if v > 0:
                    e.wait_ge(h, v)
            ins = fn(e)
            if incs is not None:
                ins.then_inc(*incs)
        self.q[eng].append(play)

    def te(self, fn, wait=(), inc=None, inc_k=1):
        self._push("tensor", fn, wait, inc, inc_k)

    def sc(self, fn, wait=(), inc=None, inc_k=1, self_wait=True):
        self._push("scalar", fn, wait, inc, inc_k, self_wait)

    def ve(self, fn, wait=(), inc=None, inc_k=1):
        self._push("vector", fn, wait, inc, inc_k)

    def sy(self, fn, wait=(), inc=None, inc_k=16):
        self._push("sync", fn, wait, inc, inc_k)

    def gp(self, fn, wait=(), inc=None, inc_k=16):
        self._push("gpsimd", fn, wait, inc, inc_k)

    def play(self):
        with self.nc.Block() as blk:
            @blk.tensor
            def _(e):
                for f in self.q["tensor"]:
                    f(e)

            @blk.scalar
            def _(e):
                for f in self.q["scalar"]:
                    f(e)

            @blk.vector
            def _(e):
                for f in self.q["vector"]:
                    f(e)

            @blk.sync
            def _(e):
                for f in self.q["sync"]:
                    f(e)

            @blk.gpsimd
            def _(e):
                for f in self.q["gpsimd"]:
                    f(e)


class SBAlloc:

    def __init__(self, nc, pool_bytes):
        self.nc = nc
        self.guard = nc.sbuf_tensor("pool", [128, pool_bytes], U8)
        self.pool = self.guard.__enter__()
        self.base = nc.lookup_mloc(self.pool).addr
        self.off = 0
        self.lim = pool_bytes
        self.stack = []
        self.uid = 0

    def alloc(self, name, shape, dtype):
        self.uid += 1
        name = f"{name}_{self.uid}"
        h = self.nc.virtual_sbuf_tensor(name, list(shape), dtype)
        per = int(np.prod(shape[1:])) * mybir.dt.size(dtype)
        addr = (self.off + 31) & ~31
        self.off = addr + per
        assert self.off <= self.lim, \
            f"SBUF pool overflow: {name} needs {per}B at {addr}, lim {self.lim}"
        self.nc._tensor(h.name, list(h.shape), h.dtype, type="SB")
        m = self.nc.lookup_mloc(h)
        m.allocated = True
        m.addr = self.base + addr
        return h

    def push(self):
        self.stack.append(self.off)

    def pop(self):
        self.off = self.stack.pop()


def build(debug=False):
    nc = bass.Bass("TRN2", target_bir_lowering=False, detect_race_conditions=False)

    x_in = nc.dram_tensor("x", [B, T, D], F32, kind="ExternalInput").ap()
    y_out = nc.dram_tensor("y", [B, T, D], F32, kind="ExternalOutput").ap()
    xmid = nc.dram_tensor("xmid", [B, T, D], F32, kind="Internal").ap()
    ssp = nc.dram_tensor("ssp", [B, T, D], BF16, kind="Internal").ap()
    xnsp = nc.dram_tensor("xnsp", [B, 128, NCH * T], BF16, kind="Internal").ap()
    if debug:
        DBGK = "ExternalOutput"
        dbg_xnt0 = nc.dram_tensor("dbg_xnt0", [B, 128, NCH * T], BF16, kind=DBGK).ap()
        dbg_s0 = nc.dram_tensor("dbg_s0", [B, T, D], BF16, kind=DBGK).ap()
        dbg_kt0 = nc.dram_tensor("dbg_kt0", [B, 128, NCH * T], BF16, kind=DBGK).ap()
        dbg_v0 = nc.dram_tensor("dbg_v0", [B, 128, TT * H * (DH + 1)], BF16, kind=DBGK).ap()
        dbg_qt0 = nc.dram_tensor("dbg_qt0", [B, 128, NCH * QB], BF16, kind=DBGK).ap()
        dbg_at0 = nc.dram_tensor("dbg_at0", [B, 128, H * QB], BF16, kind=DBGK).ap()
        dbg_g = nc.dram_tensor("dbg_g", [B, T, D], BF16, kind=DBGK).ap()
        dbg_h = nc.dram_tensor("dbg_h", [128, 256], BF16, kind=DBGK).ap()
        dbg_sg = nc.dram_tensor("dbg_sg", [128, 256], BF16, kind=DBGK).ap()
        dbg_xmid = nc.dram_tensor("dbg_xmid", [B, T, D], F32, kind=DBGK).ap()
        dbg_st = nc.dram_tensor("dbg_st", [128, 6 * TT], F32, kind=DBGK).ap()
        dbg_ub = nc.dram_tensor("dbg_ub", [128, 3 * D], BF16, kind=DBGK).ap()
        dbg_qsc = nc.dram_tensor("dbg_qsc", [128, D], F32, kind=DBGK).ap()
        dbg_rq = nc.dram_tensor("dbg_rq", [128, 4], F32, kind=DBGK).ap()

    wq = {}
    for l in (0, 1):
        for nm in ("qwT", "kwT", "vwT", "owT"):
            wq[(l, nm)] = nc.dram_tensor(f"l{l}_{nm}", [D, D], F32, kind="ExternalInput").ap()
        for nm in ("g_q", "g_k", "g_v", "b_q", "b_k", "b_v"):
            wq[(l, nm)] = nc.dram_tensor(f"l{l}_{nm}", [128, NCH], F32, kind="ExternalInput").ap()
        for nm in ("qb", "kb", "vb", "ob"):
            wq[(l, nm)] = nc.dram_tensor(f"l{l}_{nm}", [1, D], F32, kind="ExternalInput").ap()
        wq[(l, "beta")] = nc.dram_tensor(f"l{l}_beta", [1, H], F32, kind="ExternalInput").ap()
    wg_t = nc.dram_tensor("wg_t", [D, L0H], I32, kind="ExternalInput").ap()
    wu_t = nc.dram_tensor("wu_t", [D, L0H], I32, kind="ExternalInput").ap()
    wd_t = nc.dram_tensor("wd_t", [L0H, D], I32, kind="ExternalInput").ap()
    rwg = nc.dram_tensor("rwg", [D, NEXP], F32, kind="ExternalInput").ap()
    rbg = nc.dram_tensor("rbg", [1, NEXP], F32, kind="ExternalInput").ap()
    w1 = nc.dram_tensor("w1", [NEXP, D, L1H], F32, kind="ExternalInput").ap()
    b1 = nc.dram_tensor("b1", [128, NEXP * NH1], F32, kind="ExternalInput").ap()
    w2 = nc.dram_tensor("w2", [NEXP, L1H, D], F32, kind="ExternalInput").ap()
    b2 = nc.dram_tensor("b2", [1, NEXP * D], F32, kind="ExternalInput").ap()

    P = Prog(nc)
    ctxs = []

    def mksem(name):
        g = nc.semaphore(name)
        h = g.__enter__()
        ctxs.append(g)
        return Sem(h)

    SPE = mksem("spe")
    SACT = mksem("sact")
    SDVE = mksem("sdve")
    SGPC = mksem("sgpc")
    SSY = mksem("ssy")
    DW = mksem("dw")
    DX = mksem("dx")
    DS = mksem("ds")
    P.self_sems = {"vector": SDVE, "scalar": SACT}

    def barrier():
        def mkinc(S):
            def f(e):
                e.wait_ge(S.h, 0) if False else None
                return e.sem_inc(S.h, 1)
            return f
        P.sy(lambda e: e.sem_inc(SSY.h, 1), wait=[(DX, DX.n), (DS, DS.n)], inc=False)
        SSY.n += 1
        P.gp(lambda e: (e.drain(fusable=False), e.sem_inc(SGPC.h, 1))[1], wait=[(DW, DW.n)], inc=False)
        SGPC.n += 1
        P.te(lambda e: (e.drain(fusable=False), e.sem_inc(SPE.h, 1))[1], inc=False)
        SPE.n += 1
        P.sc(lambda e: (e.drain(fusable=False), e.sem_inc(SACT.h, 1))[1], inc=False)
        SACT.n += 1
        P.ve(lambda e: (e.drain(fusable=False), e.sem_inc(SDVE.h, 1))[1], inc=False)
        SDVE.n += 1
        sems = [SPE, SACT, SDVE, SGPC, SSY]
        for emitfn in (P.sy, P.gp, P.te, P.sc, P.ve):
            emitfn(lambda e: e.nop(), wait=list(sems), inc=False)

    def dxv():
        return (DX, DX.n)

    def dwv():
        return (DW, DW.n)

    psg = nc.psum_tensor("ps", [128, 4096], F32)
    ps = psg.__enter__()
    ctxs.append(psg)

    def bank(i, n=1):
        return ps[:, i * 512:(i + n) * 512]

    A = SBAlloc(nc, POOL_BYTES)
    ident = A.alloc("ident", [128, 128], BF16)
    ones_bf = A.alloc("ones_bf", [128, 128], BF16)
    ones32 = A.alloc("ones32", [128, 128], F32)
    eps_c = A.alloc("eps_c", [128, 1], F32)
    trash = A.alloc("trash", [128, 1024], BF16)
    st_scale = A.alloc("st_scale", [128, TT], F32)
    st_bias = A.alloc("st_bias", [128, TT], F32)
    st_r2 = A.alloc("st_r2", [128, TT], F32)
    st_tmp = A.alloc("st_tmp", [128, 4 * TT], F32)
    sums = A.alloc("sums", [128, TT], F32)
    sumsq = A.alloc("sumsq", [128, TT], F32)
    rk_rec = A.alloc("rk_rec", [128, TT], F32)
    beta8 = A.alloc("beta8", [128, H], F32)
    gam = A.alloc("gam", [128, 3 * NCH], F32)
    betv = A.alloc("betv", [128, 3 * NCH], F32)
    betv_bf = A.alloc("betv_bf", [128, 3 * NCH], BF16)
    w_moe = A.alloc("w_moe", [128, 2 * TT * NEXP], F32)
    b1_sb = A.alloc("b1_sb", [128, NEXP * NH1], F32)
    bgrow = A.alloc("bgrow", [1, NEXP], BF16)
    rwg_sb = A.alloc("rwg_sb", [128, NCH * NEXP], BF16)
    betarow = A.alloc("betarow", [1, H], F32)

    def emit_setup():
        io1 = trash[:].bitcast(F32)[:, 0:128]
        io2 = trash[:].bitcast(F32)[:, 128:129]
        P.gp(lambda e: e.iota(io1, [[1, 128]], channel_multiplier=0,
                              allow_small_or_imprecise_dtypes=True), inc=SGPC, inc_k=1)
        P.gp(lambda e: e.iota(io2, [[0, 1]], channel_multiplier=1,
                              allow_small_or_imprecise_dtypes=True), inc=SGPC, inc_k=1)
        P.ve(lambda e: e.memset(ones_bf[:], 1.0), inc=SDVE)
        P.ve(lambda e: e.memset(ones32[:], 1.0), inc=SDVE)
        P.ve(lambda e: e.memset(eps_c[:], EPS), inc=SDVE)
        P.ve(lambda e: e.tensor_scalar(ident[:], io1, io2, None, ALU.is_equal),
             wait=[SGPC], inc=SDVE)
        P.sy(lambda e: e.dma_start(out=b1_sb[:], in_=b1), inc=DX)
        P.gp(lambda e: e.dma_start(out=bgrow[:], in_=rbg), inc=DW)
        P.gp(lambda e: e.dma_start(
            out=rwg_sb[:].rearrange("p (c n) -> p c n", n=NEXP),
            in_=rwg.rearrange("(c p) n -> p c n", p=128)), inc=DW)
        barrier()

    def load_layer_params(l):
        for j, nm in enumerate(("g_q", "g_k", "g_v")):
            P.sy(lambda e, nm=nm, j=j: e.dma_start(
                out=gam[:, j * NCH:(j + 1) * NCH], in_=wq[(l, nm)]), inc=DX)
        for j, nm in enumerate(("b_q", "b_k", "b_v")):
            P.sy(lambda e, nm=nm, j=j: e.dma_start(
                out=betv[:, j * NCH:(j + 1) * NCH], in_=wq[(l, nm)]), inc=DX)
        P.sy(lambda e: e.dma_start(out=betarow[:], in_=wq[(l, "beta")]), inc=DX)
        barrier()
        P.ve(lambda e: e.tensor_copy(betv_bf[:], betv[:]), inc=SDVE)
        P.sc(lambda e: e.activation(betarow[:], betarow[:], AF.Exp), inc=SACT)
        P.sc(lambda e: e.activation(betarow[:], betarow[:], AF.Ln, bias=1.0), wait=[SACT], inc=SACT)
        P.ve(lambda e: e.tensor_scalar(betarow[:], betarow[:], 10.0, 0.1, ALU.min, ALU.max),
             wait=[SACT], inc=SDVE)
        P.ve(lambda e: e.tensor_scalar_mul(betarow[:], betarow[:], 0.125), inc=SDVE)
        P.te(lambda e: e.matmul(ps[:, 0:H], ones32[0:1, :], betarow[:],
                                start=True, stop=True), wait=[SDVE], inc=SPE)
        P.ve(lambda e: e.tensor_copy(beta8[:], ps[:, 0:H]), wait=[SPE], inc=SDVE)
        barrier()

    def emit_u(j, Wraw, brow_seg, ubrow, ubrow_bf, ub_rep):
        for jh in range(2):
            for c in range(NCH):
                P.te(lambda e, jh=jh, c=c, W=Wraw, j=j: e.matmul(
                    ps[0:1, 512 + jh * 512:512 + (jh + 1) * 512] if False else ps[0:1, jh * 512:(jh + 1) * 512],
                    betv_bf[:, j * NCH + c:j * NCH + c + 1],
                    W[:, c * D + jh * 512:c * D + (jh + 1) * 512],
                    start=(c == 0), stop=(c == NCH - 1)),
                    inc=SPE if c == NCH - 1 else None)
            P.ve(lambda e, jh=jh: e.tensor_copy(
                ubrow[:, jh * 512:(jh + 1) * 512], ps[0:1, jh * 512:(jh + 1) * 512]),
                wait=[SPE], inc=SDVE)
        P.ve(lambda e: e.tensor_add(ubrow[:], ubrow[:], brow_seg), wait=[dxv()], inc=SDVE)
        P.ve(lambda e: e.tensor_copy(ubrow_bf[:], ubrow[:]), inc=SDVE)
        for jh in range(2):
            P.te(lambda e, jh=jh: e.matmul(
                bank(2 + jh), ones_bf[0:1, :], ubrow_bf[:, jh * 512:(jh + 1) * 512],
                start=True, stop=True), wait=[SDVE], inc=SPE)
        P.ve(lambda e, j=j: e.tensor_copy(ub_rep[:, j * D:(j + 1) * D], bank(2, 2)),
             wait=[SPE], inc=SDVE)

    def emit_phaseA(l, s, xnT, sw, dbgA=False):
        xsrc = x_in if l == 0 else xmid
        kT, v_sb, ub_rep, betak = sw["kT"], sw["v_sb"], sw["ub_rep"], sw["betak"]

        A.push()
        kW = A.alloc("kW", [128, NCH * D], BF16)
        vW = A.alloc("vW", [128, NCH * D], BF16)
        xt = [A.alloc("xt", [128, D], F32) for _ in range(2)]
        xnsc = A.alloc("xnsc", [128, D], BF16)
        ksc = A.alloc("ksc", [128, D], F32)
        ksc_bf = A.alloc("ksc_bf", [128, D], BF16)
        brow = A.alloc("brow", [1, 2 * D], F32)
        ubrow = A.alloc("ubrow", [1, D], F32)
        ubrow_bf = A.alloc("ubrow_bf", [1, D], BF16)

        def wload(dst, src):
            P.gp(lambda e: e.dma_start(
                out=dst[:].rearrange("p (c n) -> p c n", n=D),
                in_=src.rearrange("(c p) n -> p c n", p=128)), inc=DW)
        wload(kW, wq[(l, "kwT")])
        wload(vW, wq[(l, "vwT")])
        wl_done = (DW, DW.n)
        for j, nm in enumerate(("kb", "vb")):
            P.sy(lambda e, nm=nm, j=j: e.dma_start(
                out=brow[:, j * D:(j + 1) * D], in_=wq[(l, nm)]), inc=DX)

        for tt in range(TT):
            buf = xt[tt % 2]
            P.sy(lambda e, b=buf, tt=tt, s=s: e.dma_start(
                out=b[:], in_=xsrc[s, tt * 128:(tt + 1) * 128, :]),
                wait=[SACT, SDVE] if tt >= 2 else [], inc=DX)
            P.ve(lambda e, b=buf, tt=tt: e.reduce_sum(sums[:, tt:tt + 1], b[:], axis=AX.X),
                 wait=[dxv()], inc=SDVE)
            P.sc(lambda e, b=buf, tt=tt: e.activation(
                trash[:], b[:], AF.Square, accum_out=sumsq[:, tt:tt + 1]),
                wait=[dxv()], inc=SACT)
        barrier()

        mean = st_tmp[:, 0:TT]
        var = st_tmp[:, TT:2 * TT]
        t3 = st_tmp[:, 2 * TT:3 * TT]
        P.ve(lambda e: e.tensor_scalar_mul(mean, sums[:], 1.0 / D), inc=SDVE)
        P.sc(lambda e: e.activation(var, sumsq[:], AF.Copy, scale=1.0 / D), inc=SACT)
        P.ve(lambda e: e.tensor_mul(t3, mean, mean), inc=SDVE)
        P.ve(lambda e: e.tensor_sub(var, var, t3), wait=[SACT], inc=SDVE)
        P.sc(lambda e: e.activation(t3, var, AF.Sqrt, bias=eps_c[:, 0:1]), wait=[SDVE], inc=SACT)
        P.ve(lambda e: e.reciprocal(st_scale[:], t3), wait=[SACT], inc=SDVE)
        P.ve(lambda e: e.scalar_tensor_tensor(st_bias[:], mean, -1.0, st_scale[:],
                                              ALU.mult, ALU.mult), inc=SDVE)
        P.ve(lambda e: e.tensor_mul(t3, st_scale[:], st_scale[:]), inc=SDVE)
        P.ve(lambda e: e.tensor_mul(t3, t3, var), inc=SDVE)
        P.sc(lambda e: e.activation(t3, t3, AF.Sqrt, bias=eps_c[:, 0:1]), wait=[SDVE], inc=SACT)
        P.ve(lambda e: e.reciprocal(st_r2[:], t3), wait=[SACT], inc=SDVE)
        barrier()
        if dbgA and l == 0 and s == 0:
            def _dumpst(e):
                return e.dma_start(out=dbg_st, in_=st_tmp[:, 0:0] if False else None)
            P.sy(lambda e: e.dma_start(out=dbg_st[:, 0:TT], in_=sums[:]), inc=DS)
            P.sy(lambda e: e.dma_start(out=dbg_st[:, TT:2 * TT], in_=sumsq[:]), inc=DS)
            P.sy(lambda e: e.dma_start(out=dbg_st[:, 2 * TT:3 * TT], in_=st_scale[:]), inc=DS)
            P.sy(lambda e: e.dma_start(out=dbg_st[:, 3 * TT:4 * TT], in_=st_bias[:]), inc=DS)
            P.sy(lambda e: e.dma_start(out=dbg_st[:, 4 * TT:5 * TT], in_=st_r2[:]), inc=DS)
            barrier()

        emit_u(1, kW, brow[:, 0:D], ubrow, ubrow_bf, ub_rep)
        emit_u(2, vW, brow[:, D:2 * D], ubrow, ubrow_bf, ub_rep)
        barrier()
        for j, Wt in ((1, kW), (2, vW)):
            for c in range(NCH):
                P.ve(lambda e, j=j, W=Wt, c=c: e.tensor_scalar_mul(
                    W[:, c * D:(c + 1) * D], W[:, c * D:(c + 1) * D],
                    gam[:, j * NCH + c:j * NCH + c + 1]), inc=SDVE)
        P.ve(lambda e: e.memset(
            v_sb[:].rearrange("p (t h c) -> p t h c", h=H, c=DH + 1)[:, :, :, DH:DH + 1],
            1.0), inc=SDVE)
        barrier()

        for half in range(2):
            for i in range(8):
                tt = half * 8 + i
                buf = xt[tt % 2]
                P.sy(lambda e, b=buf, tt=tt, s=s: e.dma_start(
                    out=b[:], in_=xsrc[s, tt * 128:(tt + 1) * 128, :]),
                    wait=[SDVE] if i >= 2 else [], inc=DX)
                P.ve(lambda e, b=buf, tt=tt: e.tensor_scalar(
                    xnsc[:], b[:], st_scale[:, tt:tt + 1], st_bias[:, tt:tt + 1],
                    ALU.mult, ALU.add),
                    wait=[dxv(), SPE], inc=SDVE)
                for c in range(NCH):
                    P.te(lambda e, c=c, i=i: e.transpose(
                        bank(c).bitcast(BF16)[:, i * 128:(i + 1) * 128],
                        xnsc[:, c * 128:(c + 1) * 128], ident[:]),
                        wait=[SDVE], inc=SPE if c == NCH - 1 else None)
            for c in range(NCH):
                P.ve(lambda e, c=c, half=half: e.tensor_copy(
                    xnT[:].rearrange("p (c t) -> p c t", t=T)[:, c, half * 1024:(half + 1) * 1024],
                    bank(c).bitcast(BF16)), wait=[SPE], inc=SDVE)
            barrier()

        for tt in range(TT):
            for c in range(NCH):
                for (W, bk) in ((kW, 0), (vW, 2)):
                    for jh in range(2):
                        P.te(lambda e, c=c, tt=tt, W=W, bk=bk, jh=jh: e.matmul(
                            bank(bk + jh),
                            xnT[:, c * T + tt * 128:c * T + (tt + 1) * 128],
                            W[:, c * D + jh * 512:c * D + (jh + 1) * 512],
                            start=(c == 0), stop=(c == NCH - 1)),
                            wait=[SDVE] if (c == 0 and bk == 0 and jh == 0) else [],
                            inc=SPE if (c == NCH - 1 and bk == 2 and jh == 1) else None)
            for jh in range(2):
                P.ve(lambda e, jh=jh, tt=tt: e.scalar_tensor_tensor(
                    ksc[:, jh * 512:(jh + 1) * 512], bank(jh),
                    st_r2[:, tt:tt + 1],
                    ub_rep[:, D + jh * 512:D + (jh + 1) * 512],
                    ALU.mult, ALU.add), wait=[SPE], inc=SDVE)
                P.ve(lambda e, jh=jh, tt=tt: e.scalar_tensor_tensor(
                    v_sb[:].rearrange("p (t h c) -> p t h c", h=H, c=DH + 1)
                    [:, tt, jh * 8:(jh + 1) * 8, 0:DH],
                    bank(2 + jh).rearrange("p (h c) -> p h c", c=DH),
                    st_r2[:, tt:tt + 1],
                    ub_rep[:, 2 * D + jh * 512:2 * D + (jh + 1) * 512].rearrange(
                        "o (h c) -> o h c", c=DH),
                    ALU.mult, ALU.add), inc=SDVE)
            P.sc(lambda e, tt=tt: e.activation(trash[:], ksc[:], AF.Square,
                                               accum_out=rk_rec[:, tt:tt + 1]),
                 wait=[SDVE], inc=SACT)
            P.sc(lambda e, tt=tt: e.activation(rk_rec[:, tt:tt + 1], rk_rec[:, tt:tt + 1],
                                               AF.Sqrt), wait=[SACT], inc=SACT)
            P.ve(lambda e, tt=tt: e.tensor_scalar_max(rk_rec[:, tt:tt + 1], rk_rec[:, tt:tt + 1],
                                                      EPSN), wait=[SACT], inc=SDVE)
            P.ve(lambda e, tt=tt: e.reciprocal(rk_rec[:, tt:tt + 1], rk_rec[:, tt:tt + 1]),
                 inc=SDVE)
            P.ve(lambda e, tt=tt: e.tensor_scalar_mul(ksc_bf[:], ksc[:], rk_rec[:, tt:tt + 1]),
                 inc=SDVE)
            for c in range(NCH):
                P.te(lambda e, c=c, tt=tt: e.transpose(
                    bank(4 + tt % 2).bitcast(BF16)[:, c * 128:(c + 1) * 128],
                    ksc_bf[:, c * 128:(c + 1) * 128], ident[:]),
                    wait=[SDVE], inc=SPE if c == NCH - 1 else None)
            P.ve(lambda e, tt=tt: e.tensor_copy(
                kT[:].rearrange("p (c t) -> p c t", t=T)[:, :, tt * 128:(tt + 1) * 128],
                bank(4 + tt % 2).bitcast(BF16).rearrange("p (c t) -> p c t", t=128)),
                wait=[SPE], inc=SDVE)
        barrier()

        if l == 1:
            pcol = st_tmp[:, 3 * TT:3 * TT + NEXP]
            mcol = st_tmp[:, 3 * TT + NEXP:3 * TT + 2 * NEXP]
            for tt in range(TT):
                for c in range(NCH):
                    P.te(lambda e, c=c, tt=tt: e.matmul(
                        ps[:, 0:NEXP],
                        xnT[:, c * T + tt * 128:c * T + (tt + 1) * 128],
                        rwg_sb[:, c * NEXP:(c + 1) * NEXP],
                        start=(c == 0), stop=False),
                        wait=[SACT] if c == 0 else [], inc=None)
                P.te(lambda e: e.matmul(ps[:, 0:NEXP], ones_bf[0:1, :], bgrow[:],
                                        start=False, stop=True), inc=SPE)
                P.sc(lambda e, pcol=pcol: e.activation(pcol, ps[:, 0:NEXP], AF.Sigmoid),
                     wait=[SPE], inc=SACT)
                P.ve(lambda e, pcol=pcol, mcol=mcol: e.tensor_scalar(
                    mcol, pcol, 0.5, None, ALU.is_gt), wait=[SACT], inc=SDVE)
                P.ve(lambda e, s=s, tt=tt, pcol=pcol, mcol=mcol: e.tensor_tensor(
                    w_moe[:, (s * TT + tt) * NEXP:(s * TT + tt + 1) * NEXP],
                    pcol, mcol, ALU.mult), inc=SDVE)
        barrier()
        A.pop()

    def emit_attn(l, s, xnT, sw, dbg=None):
        xsrc = x_in if l == 0 else xmid
        kT, v_sb, ub_rep, betak = sw["kT"], sw["v_sb"], sw["ub_rep"], sw["betak"]
        A.push()
        qW = A.alloc("qW", [128, NCH * D], BF16)
        oW = A.alloc("oW", [64, H * D], BF16)
        qTb = A.alloc("qTb", [128, NCH * QB], BF16)
        E_sb = A.alloc("E_sb", [128, 4096], BF16)
        attnT = A.alloc("attnT", [128, H * QB], BF16)
        rs_rec = A.alloc("rs_rec", [128, 512], F32)
        obrow = A.alloc("obrow", [1, D], BF16)

        A.push()
        brow_q = A.alloc("brow_q", [1, D], F32)
        ubrow = A.alloc("ubrow_a", [1, D], F32)
        ubrow_bf = A.alloc("ubrow_abf", [1, D], BF16)

        P.gp(lambda e: e.dma_start(
            out=qW[:].rearrange("p (c n) -> p c n", n=D),
            in_=wq[(l, "qwT")].rearrange("(c p) n -> p c n", p=128)), inc=DW)
        P.gp(lambda e: e.dma_start(
            out=oW[:].rearrange("p (h n) -> p h n", n=D),
            in_=wq[(l, "owT")].rearrange("(h p) n -> p h n", p=64)), inc=DW)
        P.gp(lambda e: e.dma_start(out=obrow[:], in_=wq[(l, "ob")]), inc=DW)
        P.sy(lambda e: e.dma_start(out=brow_q[:], in_=wq[(l, "qb")]), inc=DX)
        P.ve(lambda e: e.memset(rs_rec[:], 0.0), inc=SDVE)
        barrier()
        emit_u(0, qW, brow_q[:], ubrow, ubrow_bf, ub_rep)
        barrier()
        for c in range(NCH):
            P.ve(lambda e, c=c: e.tensor_scalar_mul(
                qW[:, c * D:(c + 1) * D], qW[:, c * D:(c + 1) * D],
                gam[:, c:c + 1]), inc=SDVE)
        barrier()
        if dbg is not None and l == 0 and s == 0:
            P.sy(lambda e, ub=ub_rep: e.dma_start(out=dbg_ub, in_=ub[:]), inc=DS)
            barrier()
        A.pop()

        if dbg is not None:
            P.ve(lambda e: e.memset(qTb[:], 0.0), inc=SDVE)
            P.ve(lambda e: e.memset(attnT[:], 0.0), inc=SDVE)
            barrier()
        qsc = A.alloc("qsc", [128, D], F32)
        qsc_bf = A.alloc("qsc_bf", [128, D], BF16)
        rqss = A.alloc("rqss", [128, 4], F32)
        xe = [A.alloc("xe", [128, D], F32)]
        stile = A.alloc("stile2", [128, D], BF16)

        pv_bank_free = [0, 0]
        bc_mark = [0]
        epi_mark = [0]

        def emit_oproj_tile(oqb, i):
            tt = oqb * 4 + i
            P.sy(lambda e, i=i, tt=tt, s=s: e.dma_start(
                out=xe[0][:], in_=xsrc[s, tt * 128:(tt + 1) * 128, :]),
                wait=[(SDVE, epi_mark[0])], inc=DX)
            for jh in range(2):
                bk = 6 + jh
                for h in range(H):
                    wl = [(SDVE, epi_mark[0])] if (h == 0 and jh == 0) else []
                    P.te(lambda e, h=h, i=i, jh=jh, bk=bk, oqb=oqb: e.matmul(
                        bank(bk), attnT[0:64, h * QB + i * 128:h * QB + (i + 1) * 128],
                        oW[0:64, h * D + jh * 512:h * D + (jh + 1) * 512],
                        start=(h == 0), stop=False), wait=wl, inc=None)
                P.te(lambda e, jh=jh, bk=bk: e.matmul(
                    bank(bk), ones_bf[0:1, :], obrow[:, jh * 512:(jh + 1) * 512],
                    start=False, stop=True), inc=SPE)
            P.ve(lambda e, tt=tt: e.tensor_scalar(
                xe[0][:], xe[0][:], st_scale[:, tt:tt + 1], st_bias[:, tt:tt + 1],
                ALU.mult, ALU.add), wait=[dxv(), SPE], inc=SDVE)
            for jh in range(2):
                P.ve(lambda e, jh=jh: e.tensor_add(
                    stile[:, jh * 512:(jh + 1) * 512], bank(6 + jh),
                    xe[0][:, jh * 512:(jh + 1) * 512]),
                    wait=[(DS, DS.n)] if jh == 0 else [], inc=SDVE)
            epi_mark[0] = SDVE.n
            P.sy(lambda e, tt=tt, s=s: e.dma_start(
                out=ssp[s, tt * 128:(tt + 1) * 128, :], in_=stile[:]),
                wait=[SDVE], inc=DS)
        for qb in range(NQB):
            for i in range(4):
                tt = qb * 4 + i
                for c in range(NCH):
                    for jh in range(2):
                        P.te(lambda e, c=c, tt=tt, jh=jh: e.matmul(
                            bank(jh), xnT[:, c * T + tt * 128:c * T + (tt + 1) * 128],
                            qW[:, c * D + jh * 512:c * D + (jh + 1) * 512],
                            start=(c == 0), stop=(c == NCH - 1)),
                            wait=[SDVE] if (c == 0 and jh == 0) else [],
                            inc=SPE if (c == NCH - 1 and jh == 1) else None)
                for jh in range(2):
                    P.ve(lambda e, jh=jh, tt=tt: e.scalar_tensor_tensor(
                        qsc[:, jh * 512:(jh + 1) * 512], bank(jh),
                        st_r2[:, tt:tt + 1], ub_rep[:, jh * 512:(jh + 1) * 512],
                        ALU.mult, ALU.add), wait=[SPE], inc=SDVE)
                P.sc(lambda e, i=i: e.activation(trash[:], qsc[:], AF.Square,
                                                 accum_out=rqss[:, i:i + 1]),
                     wait=[SDVE], inc=SACT)
                P.sc(lambda e, i=i: e.activation(rqss[:, i:i + 1], rqss[:, i:i + 1],
                                                 AF.Sqrt), wait=[SACT], inc=SACT)
                P.ve(lambda e, i=i: e.tensor_scalar_max(rqss[:, i:i + 1], rqss[:, i:i + 1],
                                                        EPSN), wait=[SACT], inc=SDVE)
                P.ve(lambda e, i=i: e.reciprocal(rqss[:, i:i + 1], rqss[:, i:i + 1]),
                     inc=SDVE)
                P.ve(lambda e, i=i: e.tensor_scalar_mul(qsc_bf[:], qsc[:], rqss[:, i:i + 1]),
                     inc=SDVE)
                if dbg is not None and l == 0 and s == 0 and qb == 0 and i == 0:
                    P.sy(lambda e: e.dma_start(out=dbg_qsc, in_=qsc[:]), wait=[SDVE], inc=DS)
                    P.sy(lambda e: e.dma_start(out=dbg_rq[:, 0:1], in_=rqss[:, 0:1], allow_slow_non_contiguous=True), wait=[SDVE], inc=DS)
                    barrier()
                for c in range(NCH):
                    P.te(lambda e, c=c, i=i: e.transpose(
                        bank(2 + i % 2).bitcast(BF16)[:, c * 128:(c + 1) * 128],
                        qsc_bf[:, c * 128:(c + 1) * 128], ident[:]),
                        wait=[SDVE], inc=SPE if c == NCH - 1 else None)
                P.ve(lambda e, i=i: e.tensor_copy(
                    qTb[:].rearrange("p (c t) -> p c t", t=QB)[:, :, i * 128:(i + 1) * 128],
                    bank(2 + i % 2).bitcast(BF16).rearrange("p (c t) -> p c t", t=128)),
                    wait=[SPE], inc=SDVE)
                if qb >= 1:
                    emit_oproj_tile(qb - 1, i)
            barrier()
            if dbg is not None and qb == 0:
                P.sy(lambda e, s=s: e.dma_start(out=dbg[0][s], in_=qTb[:]), inc=DS)
                barrier()

            for pr in range(H // 2):
                c = pr
                s_marks = [[], []]
                exp_marks = [[], []]
                pv_marks = [[], []]

                def emit_S_k2(hh, ktg, k2):
                    rb = hh * 64
                    sbp = hh * 2
                    kt = 2 * ktg + k2
                    wl = []
                    if ktg >= 1 and k2 == 0:
                        wl.append((SACT, exp_marks[hh][ktg - 1]))
                    P.te(lambda e, c=c, rb=rb, sbp=sbp, kt=kt, k2=k2: e.matmul(
                        ps[:, (sbp + k2) * 512:(sbp + k2 + 1) * 512],
                        kT[rb:rb + 64, c * T + kt * 128:c * T + (kt + 1) * 128],
                        qTb[rb:rb + 64, c * QB:(c + 1) * QB],
                        start=True, stop=True, tile_position=(rb, 0)),
                        wait=wl, inc=SPE)

                def emit_exp(hh, ktg):
                    h = 2 * pr + hh
                    sbp = hh * 2
                    wl = [(SPE, s_marks[hh][ktg])]
                    if ktg >= 2:
                        wl.append((SPE, pv_marks[hh][ktg - 2]))
                    P.sc(lambda e, sbp=sbp, h=h, hh=hh, ktg=ktg: e.activation(
                        E_sb[:, hh * 2048 + (ktg % 2) * 1024:hh * 2048 + (ktg % 2) * 1024 + 1024],
                        ps[:, sbp * 512:(sbp + 2) * 512], AF.Exp,
                        scale=beta8[:, h:h + 1]),
                        wait=wl)
                    exp_marks[hh].append(SACT.n)

                def emit_PV(hh, ktg):
                    h = 2 * pr + hh
                    for k2 in range(2):
                        kt = 2 * ktg + k2
                        wl = [(SACT, exp_marks[hh][ktg])]
                        if kt == 0:
                            wl.append((SDVE, pv_bank_free[hh]))
                        P.te(lambda e, kt=kt, h=h, hh=hh, ktg=ktg, k2=k2: e.matmul(
                            ps[0:65, (4 + hh) * 512:(5 + hh) * 512],
                            v_sb[:].rearrange("p (t h c) -> p t h c", h=H, c=DH + 1)[:, kt, h, :],
                            E_sb[:, hh * 2048 + (ktg % 2) * 1024 + k2 * 512:
                                 hh * 2048 + (ktg % 2) * 1024 + (k2 + 1) * 512],
                            start=(kt == 0), stop=(kt == TT - 1)),
                            wait=wl, inc=SPE if k2 == 1 else None)
                    pv_marks[hh].append(SPE.n)

                for ktg in range(8):
                    emit_S_k2(0, ktg, 0)
                    emit_S_k2(1, ktg, 0)
                    emit_S_k2(0, ktg, 1)
                    emit_S_k2(1, ktg, 1)
                    for hh in range(2):
                        s_marks[hh].append(SPE.n)
                    for hh in range(2):
                        emit_exp(hh, ktg)
                    if ktg >= 1:
                        for hh in range(2):
                            emit_PV(hh, ktg - 1)
                for hh in range(2):
                    emit_PV(hh, 7)
                for hh in range(2):
                    h = 2 * pr + hh
                    P.ve(lambda e, h=h, hh=hh: e.tensor_copy(
                        attnT[0:64, h * QB:(h + 1) * QB],
                        ps[0:64, (4 + hh) * 512:(5 + hh) * 512]), wait=[SPE], inc=SDVE)
                    P.ve(lambda e, hh=hh: e.reciprocal(
                        rs_rec[64:65, :], ps[64:65, (4 + hh) * 512:(5 + hh) * 512]),
                        wait=[(SPE, bc_mark[0])], inc=SDVE)
                    P.te(lambda e, hh=hh: e.matmul(
                        bank(4 + hh), ones32[64:65, :], rs_rec[64:65, :],
                        start=True, stop=True), wait=[SDVE], inc=SPE)
                    bc_mark[0] = SPE.n
                    P.ve(lambda e, h=h, hh=hh: e.tensor_tensor(
                        attnT[0:64, h * QB:(h + 1) * QB],
                        attnT[0:64, h * QB:(h + 1) * QB],
                        ps[0:64, (4 + hh) * 512:(5 + hh) * 512], ALU.mult),
                        wait=[(SPE, bc_mark[0])], inc=SDVE)
                    pv_bank_free[hh] = SDVE.n
            barrier()
            if dbg is not None and qb == 0:
                P.sy(lambda e, s=s: e.dma_start(out=dbg[1][s], in_=attnT[:]), inc=DS)
                barrier()

            if qb == NQB - 1:
                for i in range(4):
                    emit_oproj_tile(qb, i)
            barrier()
        if l == 1:
            P.sy(lambda e, s=s: e.dma_start(out=xnsp[s], in_=xnT[:]), inc=DS)
            barrier()
        A.pop()

    def emit_ffn0(s, xnT, dbg=False):
        A.push()
        wgS = A.alloc("wgS", [128, NCH * L0H], BF16)
        wuS = A.alloc("wuS", [128, NCH * L0H], BF16)
        wdS = A.alloc("wdS", [128, NH0 * D], BF16)
        sgb = [A.alloc("sgb", [128, 256], BF16) for _ in range(2)]
        hb = [A.alloc("hb", [128, 256], BF16) for _ in range(2)]
        gxe = A.alloc("gxe", [128, D], F32)
        gse = A.alloc("gse", [128, D], BF16)
        ggb = A.alloc("ggb", [128, D], BF16)
        gob = A.alloc("gob", [128, D], F32)

        P.gp(lambda e: e.dma_start(
            out=wgS[:].rearrange("p (c n) -> p c n", n=L0H),
            in_=wg_t.rearrange("(c p) n -> p c n", p=128)), inc=DW)
        P.gp(lambda e: e.dma_start(
            out=wuS[:].rearrange("p (c n) -> p c n", n=L0H),
            in_=wu_t.rearrange("(c p) n -> p c n", p=128)), inc=DW)
        P.gp(lambda e: e.dma_start(
            out=wdS[:].rearrange("p (c n) -> p c n", n=D),
            in_=wd_t.rearrange("(c p) n -> p c n", p=128)), inc=DW)
        barrier()

        for tq in range(T // 256):
            stt_mark = {}
            silu_mark = {}
            down_mark = {}

            def emit_gu(ht):
                gb, ub2 = ht % 2, 2 + ht % 2
                for c in range(NCH):
                    P.te(lambda e, c=c, ht=ht, tq=tq, gb=gb: e.matmul(
                        ps[:, gb * 512:gb * 512 + 256],
                        wgS[:, c * L0H + ht * 128:c * L0H + (ht + 1) * 128],
                        xnT[:].rearrange("p (c t) -> p c t", t=T)[:, c, tq * 256:(tq + 1) * 256],
                        start=(c == 0), stop=(c == NCH - 1)),
                        wait=[(SACT, silu_mark[ht - 2])] if (ht >= 2 and c == 0) else [],
                        inc=None)
                for c in range(NCH):
                    P.te(lambda e, c=c, ht=ht, tq=tq, ub2=ub2: e.matmul(
                        ps[:, ub2 * 512:ub2 * 512 + 256],
                        wuS[:, c * L0H + ht * 128:c * L0H + (ht + 1) * 128],
                        xnT[:].rearrange("p (c t) -> p c t", t=T)[:, c, tq * 256:(tq + 1) * 256],
                        start=(c == 0), stop=(c == NCH - 1)),
                        wait=[(SDVE, stt_mark[ht - 2])] if (ht >= 2 and c == 0) else [],
                        inc=SPE if c == NCH - 1 else None)
                P.sc(lambda e, gb=gb: e.activation(
                    sgb[ht % 2][:], ps[:, gb * 512:gb * 512 + 256], AF.Silu, scale=1.0 / 127.0),
                    wait=[SPE, (SDVE, stt_mark.get(ht - 2, 0))], self_wait=False)
                silu_mark[ht] = SACT.n
                P.ve(lambda e, ub2=ub2, ht=ht: e.scalar_tensor_tensor(
                    hb[ht % 2][:], ps[:, ub2 * 512:ub2 * 512 + 256], 1.0 / 127.0,
                    sgb[ht % 2][:], ALU.mult, ALU.mult),
                    wait=[(SACT, silu_mark[ht]), (SPE, down_mark.get(ht - 2, 0))], inc=SDVE)
                stt_mark[ht] = SDVE.n

            def emit_down(ht):
                for tsub in range(2):
                    for jh in range(2):
                        P.te(lambda e, ht=ht, tsub=tsub, jh=jh: e.matmul(
                            bank(4 + tsub * 2 + jh),
                            hb[ht % 2][:, tsub * 128:(tsub + 1) * 128],
                            wdS[:, ht * D + jh * 512:ht * D + (jh + 1) * 512],
                            start=(ht == 0), stop=(ht == NH0 - 1)),
                            wait=[(SDVE, stt_mark[ht])] if (tsub == 0 and jh == 0) else [],
                            inc=SPE if (tsub == 1 and jh == 1) else None)
                down_mark[ht] = SPE.n

            for ht in range(NH0):
                emit_gu(ht)
                if ht >= 1:
                    emit_down(ht - 1)
            emit_down(NH0 - 1)
            for tsub in range(2):
                tt = tq * 2 + tsub
                P.sy(lambda e, s=s, tt=tt: e.dma_start(
                    out=gxe[:], in_=x_in[s, tt * 128:(tt + 1) * 128, :]),
                    wait=[SDVE] if tsub == 1 else [], inc=DX)
                P.sy(lambda e, s=s, tt=tt: e.dma_start(
                    out=gse[:], in_=ssp[s, tt * 128:(tt + 1) * 128, :]), inc=DX)
                for jh in range(2):
                    P.sc(lambda e, tsub=tsub, jh=jh: e.activation(
                        ggb[:, jh * 512:(jh + 1) * 512], bank(4 + tsub * 2 + jh),
                        AF.Silu, scale=1.0 / 127.0), wait=[SPE, SDVE], inc=SACT)
                if dbg:
                    P.sy(lambda e, s=s, tt=tt: e.dma_start(
                        out=dbg_g[s, tt * 128:(tt + 1) * 128, :], in_=ggb[:]),
                        wait=[SACT], inc=DS)
                    barrier()
                P.ve(lambda e: e.tensor_mul(ggb[:], gse[:], ggb[:]),
                     wait=[SACT, dxv()], inc=SDVE)
                P.ve(lambda e: e.tensor_add(gob[:], gxe[:], ggb[:]),
                     wait=[(DS, DS.n)], inc=SDVE)
                P.sy(lambda e, s=s, tt=tt: e.dma_start(
                    out=xmid[s, tt * 128:(tt + 1) * 128, :], in_=gob[:]),
                    wait=[SDVE], inc=DS)
            barrier()
        A.pop()

    def emit_ffn1():
        A.push()
        w1S = A.alloc("w1S", [128, NCH * L1H], BF16)
        w2S = A.alloc("w2S", [128, NH1 * D], BF16)
        acc = A.alloc("acc", [128, 8 * D], F32)
        xnh = A.alloc("xnh", [128, NCH * 1024], BF16)
        b2e = A.alloc("b2e", [1, D], BF16)
        hb = [A.alloc("hbm", [128, 256], BF16) for _ in range(2)]
        gxe = A.alloc("gxe1", [128, D], F32)
        gse = A.alloc("gse1", [128, D], BF16)
        ggb = A.alloc("ggb1", [128, D], BF16)
        gob = A.alloc("gob1", [128, D], F32)

        w1c_reader = [0] * NCH
        w2g_reader = [0] * 8
        b2_reader = [0]
        for sw_i in range(4):
            s, half = sw_i // 2, sw_i % 2
            P.sy(lambda e, s=s, half=half: e.dma_start(
                out=xnh[:].rearrange("p (c t) -> p c t", t=1024),
                in_=xnsp[s].rearrange("p (c t) -> p c t", t=T)[:, :, half * 1024:(half + 1) * 1024]),
                inc=DX)
            barrier()
            acc_mark = [0] * 8
            last_acc = [0]
            for e_ in range(NEXP):
                w1c_mark = []
                for cc in range(NCH):
                    P.gp(lambda e, e_=e_, cc=cc: e.dma_start(
                        out=w1S[:, cc * L1H:(cc + 1) * L1H],
                        in_=w1[e_, cc * 128:(cc + 1) * 128, :]),
                        wait=[(SPE, w1c_reader[cc])], inc=DW)
                    w1c_mark.append(DW.n)
                w2g_mark = []
                for gg in range(8):
                    P.gp(lambda e, e_=e_, gg=gg: e.dma_start(
                        out=w2S[:, gg * 4096:(gg + 1) * 4096],
                        in_=w2[e_, gg * 512:(gg + 1) * 512, :]),
                        wait=[(SPE, w2g_reader[gg])], inc=DW)
                    w2g_mark.append(DW.n)
                P.gp(lambda e, e_=e_: e.dma_start(
                    out=b2e[:], in_=b2[0:1, e_ * D:(e_ + 1) * D]),
                    wait=[(SPE, b2_reader[0])], inc=DW)
                b2_mark = DW.n
                for tq in range(4):
                    relu_mark = {}

                    def emit_w1(ht):
                        pb = ht % 2
                        for c in range(NCH):
                            wl = []
                            if c == 0 and ht >= 2:
                                wl.append((SACT, relu_mark[ht - 2]))
                            if tq == 0:
                                wl.append((DW, w1c_mark[c]))
                            P.te(lambda e, c=c, ht=ht, tq=tq, pb=pb: e.matmul(
                                ps[:, pb * 512:pb * 512 + 256],
                                w1S[:, c * L1H + ht * 128:c * L1H + (ht + 1) * 128],
                                xnh[:, c * 1024 + tq * 256:c * 1024 + (tq + 1) * 256],
                                start=(c == 0), stop=(c == NCH - 1)),
                                wait=wl, inc=SPE if c == NCH - 1 else None)
                        P.sc(lambda e, pb=pb, ht=ht, e_=e_: e.activation(
                            hb[ht % 2][:], ps[:, pb * 512:pb * 512 + 256], AF.Relu,
                            bias=b1_sb[:, e_ * NH1 + ht:e_ * NH1 + ht + 1]),
                            wait=[SPE], self_wait=False)
                        relu_mark[ht] = SACT.n

                    def emit_w2(ht):
                        for tsub in range(2):
                            for jh in range(2):
                                wl = [(SACT, relu_mark[ht])] if (tsub == 0 and jh == 0) else []
                                if ht == 0 and tsub == 0 and jh == 0:
                                    wl.append((SDVE, last_acc[0]))
                                if tq == 0 and ht % 4 == 0 and tsub == 0 and jh == 0:
                                    wl.append((DW, w2g_mark[ht // 4]))
                                P.te(lambda e, ht=ht, tsub=tsub, jh=jh: e.matmul(
                                    bank(4 + tsub * 2 + jh),
                                    hb[ht % 2][:, tsub * 128:(tsub + 1) * 128],
                                    w2S[:, ht * D + jh * 512:ht * D + (jh + 1) * 512],
                                    start=(ht == 0), stop=False),
                                    wait=wl, inc=None)

                    for ht in range(NH1):
                        emit_w1(ht)
                        if ht >= 1:
                            emit_w2(ht - 1)
                    emit_w2(NH1 - 1)
                    for c in range(NCH):
                        w1c_reader[c] = SPE.n
                    for tsub in range(2):
                        for jh in range(2):
                            P.te(lambda e, tsub=tsub, jh=jh, e_=e_: e.matmul(
                                bank(4 + tsub * 2 + jh), ones_bf[0:1, :],
                                b2e[:, jh * 512:(jh + 1) * 512],
                                start=False, stop=True),
                                wait=[(DW, b2_mark)] if (tsub == 0 and jh == 0) else [],
                                inc=SPE if (tsub == 1 and jh == 1) else None)
                    if tq == 3:
                        b2_reader[0] = SPE.n
                        for gg in range(8):
                            w2g_reader[gg] = SPE.n
                    for tsub in range(2):
                        gt = tq * 2 + tsub
                        wcol = w_moe[:, (s * TT + half * 8 + gt) * NEXP + e_:
                                     (s * TT + half * 8 + gt) * NEXP + e_ + 1]
                        at = acc[:, gt * D:(gt + 1) * D]
                        for jh in range(2):
                            src2 = bank(4 + tsub * 2 + jh)
                            dst = at[:, jh * 512:(jh + 1) * 512]
                            if e_ == 0:
                                P.ve(lambda e, dst=dst, src=src2, wcol=wcol: e.tensor_scalar_mul(
                                    dst, src, wcol), wait=[SPE], inc=SDVE)
                            else:
                                P.ve(lambda e, dst=dst, src=src2, wcol=wcol: e.scalar_tensor_tensor(
                                    dst, src, wcol, dst, ALU.mult, ALU.add),
                                    wait=[SPE], inc=SDVE)
                    acc_mark[tq * 2] = SDVE.n
                    acc_mark[tq * 2 + 1] = SDVE.n
                    last_acc[0] = SDVE.n
            barrier()
            for i in range(8):
                tt = half * 8 + i
                P.sy(lambda e, s=s, tt=tt: e.dma_start(
                    out=gxe[:], in_=xmid[s, tt * 128:(tt + 1) * 128, :]),
                    wait=[SDVE] if i >= 1 else [], inc=DX)
                P.sy(lambda e, s=s, tt=tt: e.dma_start(
                    out=gse[:], in_=ssp[s, tt * 128:(tt + 1) * 128, :]), inc=DX)
                P.sc(lambda e, i=i: e.activation(ggb[:], acc[:, i * D:(i + 1) * D], AF.Silu),
                     wait=[dxv(), SDVE])
                P.ve(lambda e: e.tensor_mul(ggb[:], gse[:], ggb[:]), wait=[SACT], inc=SDVE)
                P.ve(lambda e: e.tensor_add(gob[:], gxe[:], ggb[:]),
                     wait=[(DS, DS.n)], inc=SDVE)
                P.sy(lambda e, s=s, tt=tt: e.dma_start(
                    out=y_out[s, tt * 128:(tt + 1) * 128, :], in_=gob[:]),
                    wait=[SDVE], inc=DS)
            barrier()
        A.pop()

    emit_setup()
    for l in (0, 1):
        load_layer_params(l)
        for s in range(B):
            A.push()
            xnT = A.alloc("xnT", [128, NCH * T], BF16)
            A.push()
            sw = {
                "kT": A.alloc("kT", [128, NCH * T], BF16),
                "v_sb": A.alloc("v_sb", [128, TT * H * (DH + 1)], BF16),
                "ub_rep": A.alloc("ub_rep", [128, 3 * D], BF16),
                "betak": A.alloc("betak", [128, TT * H], F32),
            }
            emit_phaseA(l, s, xnT, sw, dbgA=debug)
            if debug and l == 0:
                P.sy(lambda e, s=s, xnT=xnT: e.dma_start(out=dbg_xnt0[s], in_=xnT[:]), inc=DS)
                P.sy(lambda e, s=s, kt=sw["kT"]: e.dma_start(out=dbg_kt0[s], in_=kt[:]), inc=DS)
                P.sy(lambda e, s=s, vv=sw["v_sb"]: e.dma_start(out=dbg_v0[s], in_=vv[:]), inc=DS)
                barrier()
            emit_attn(l, s, xnT, sw, dbg=(dbg_qt0, dbg_at0) if (debug and l == 0) else None)
            A.pop()
            if l == 0:
                emit_ffn0(s, xnT, dbg=debug)
                if debug:
                    P.sy(lambda e, s=s: e.dma_start(out=dbg_s0[s], in_=ssp[s]), inc=DS)
                    barrier()
            A.pop()
        if l == 1:
            emit_ffn1()

    if debug:
        P.sy(lambda e: e.dma_start(out=dbg_xmid, in_=xmid), inc=DS)
    P.sy(lambda e: e.nop(), wait=[(DS, DS.n), (DX, DX.n), (DW, DW.n)])
    P.play()
    return nc


def _prep_inputs(inputs):
    f = np.ascontiguousarray

    def pm(vec):
        return f(vec.reshape(-1, 128).T)

    m = {}
    for l in (0, 1):
        p = f"L{l}_attn_"
        m[f"l{l}_qwT"] = f(inputs[p + "q_w"].T)
        m[f"l{l}_kwT"] = f(inputs[p + "k_w"].T)
        m[f"l{l}_vwT"] = f(inputs[p + "v_w"].T)
        m[f"l{l}_owT"] = f(inputs[p + "o_w"].T)
        m[f"l{l}_g_q"] = pm(inputs[p + "ln_q_g"])
        m[f"l{l}_g_k"] = pm(inputs[p + "ln_k_g"])
        m[f"l{l}_g_v"] = pm(inputs[p + "ln_v_g"])
        m[f"l{l}_b_q"] = pm(inputs[p + "ln_q_b"])
        m[f"l{l}_b_k"] = pm(inputs[p + "ln_k_b"])
        m[f"l{l}_b_v"] = pm(inputs[p + "ln_v_b"])
        m[f"l{l}_qb"] = f(inputs[p + "q_b"].reshape(1, -1))
        m[f"l{l}_kb"] = f(inputs[p + "k_b"].reshape(1, -1))
        m[f"l{l}_vb"] = f(inputs[p + "v_b"].reshape(1, -1))
        m[f"l{l}_ob"] = f(inputs[p + "o_b"].reshape(1, -1))
        m[f"l{l}_beta"] = f(inputs[p + "beta_param"].reshape(1, -1))
    m["wg_t"] = f(inputs["L0_ffn_w_gate"].T)
    m["wu_t"] = f(inputs["L0_ffn_w_up"].T)
    m["wd_t"] = f(inputs["L0_ffn_w_down"].T)
    m["rwg"] = f(inputs["L1_ffn_Wg"])
    m["rbg"] = f(inputs["L1_ffn_bg"].reshape(1, -1))
    m["w1"] = f(inputs["L1_ffn_W1"])
    b1v = inputs["L1_ffn_b1"]
    m["b1"] = f(b1v.reshape(NEXP, NH1, 128).transpose(2, 0, 1).reshape(128, -1))
    m["w2"] = f(inputs["L1_ffn_W2"])
    m["b2"] = f(inputs["L1_ffn_b2"].reshape(1, -1))
    return m


def kernel(**inputs):
    x = np.ascontiguousarray(inputs["L0_attn_query"])
    shared = _prep_inputs(inputs)
    nc = build()
    in_maps = []
    for c in range(8):
        im = dict(shared)
        im["x"] = np.ascontiguousarray(x[c * B:(c + 1) * B])
        in_maps.append(im)
    res = run_bass_kernel_spmd(nc, in_maps, core_ids=list(range(8)))
    return np.concatenate([r["y"] for r in res.results], axis=0)


# version 09
# speedup vs baseline: 1.1057x; 1.1026x over previous
import numpy as np
import ml_dtypes
import concourse.bass as bass
import concourse.mybir as mybir
from concourse.bass_utils import run_bass_kernel_spmd

F32 = mybir.dt.float32
BF16 = mybir.dt.bfloat16
I32 = mybir.dt.int32
U8 = mybir.dt.uint8
AF = mybir.ActivationFunctionType
ALU = mybir.AluOpType
AX = mybir.AxisListType

B = 2
T = 2048
D = 1024
H = 16
DH = 64
TT = T // 128
NCH = D // 128
QB = 512
NQB = T // QB
L0H = 3072
NH0 = L0H // 128
L1H = 4096
NH1 = L1H // 128
NEXP = 8
EPS = 1e-5
EPSN = 1e-6

POOL_BYTES = 210944


class Sem:
    def __init__(self, h):
        self.h = h
        self.n = 0


class Prog:

    def __init__(self, nc):
        self.nc = nc
        self.q = {e: [] for e in ("tensor", "scalar", "vector", "sync", "gpsimd")}
        self.self_sems = {}

    def _push(self, eng, fn, wait, inc, inc_k, self_wait=True):
        waits = []
        ss = self.self_sems.get(eng)
        if ss is not None and self_wait:
            waits.append((ss.h, ss.n))
        for w in wait:
            if isinstance(w, tuple):
                waits.append((w[0].h, w[1]))
            else:
                waits.append((w.h, w.n))
        if inc is None and ss is not None:
            inc = ss
            inc_k = 1
        if inc is False:
            inc = None
        if inc is not None:
            inc.n += inc_k
            incs = (inc.h, inc_k)
        else:
            incs = None

        def play(e):
            for h, v in waits:
                if v > 0:
                    e.wait_ge(h, v)
            ins = fn(e)
            if incs is not None:
                ins.then_inc(*incs)
        self.q[eng].append(play)

    def te(self, fn, wait=(), inc=None, inc_k=1):
        self._push("tensor", fn, wait, inc, inc_k)

    def sc(self, fn, wait=(), inc=None, inc_k=1, self_wait=True):
        self._push("scalar", fn, wait, inc, inc_k, self_wait)

    def ve(self, fn, wait=(), inc=None, inc_k=1):
        self._push("vector", fn, wait, inc, inc_k)

    def sy(self, fn, wait=(), inc=None, inc_k=16):
        self._push("sync", fn, wait, inc, inc_k)

    def gp(self, fn, wait=(), inc=None, inc_k=16):
        self._push("gpsimd", fn, wait, inc, inc_k)

    def play(self):
        with self.nc.Block() as blk:
            @blk.tensor
            def _(e):
                for f in self.q["tensor"]:
                    f(e)

            @blk.scalar
            def _(e):
                for f in self.q["scalar"]:
                    f(e)

            @blk.vector
            def _(e):
                for f in self.q["vector"]:
                    f(e)

            @blk.sync
            def _(e):
                for f in self.q["sync"]:
                    f(e)

            @blk.gpsimd
            def _(e):
                for f in self.q["gpsimd"]:
                    f(e)


class SBAlloc:

    def __init__(self, nc, pool_bytes):
        self.nc = nc
        self.guard = nc.sbuf_tensor("pool", [128, pool_bytes], U8)
        self.pool = self.guard.__enter__()
        self.base = nc.lookup_mloc(self.pool).addr
        self.off = 0
        self.lim = pool_bytes
        self.stack = []
        self.uid = 0

    def alloc(self, name, shape, dtype):
        self.uid += 1
        name = f"{name}_{self.uid}"
        h = self.nc.virtual_sbuf_tensor(name, list(shape), dtype)
        per = int(np.prod(shape[1:])) * mybir.dt.size(dtype)
        addr = (self.off + 31) & ~31
        self.off = addr + per
        assert self.off <= self.lim, \
            f"SBUF pool overflow: {name} needs {per}B at {addr}, lim {self.lim}"
        self.nc._tensor(h.name, list(h.shape), h.dtype, type="SB")
        m = self.nc.lookup_mloc(h)
        m.allocated = True
        m.addr = self.base + addr
        return h

    def push(self):
        self.stack.append(self.off)

    def pop(self):
        self.off = self.stack.pop()


def build(debug=False):
    nc = bass.Bass("TRN2", target_bir_lowering=False, detect_race_conditions=False)

    x_in = nc.dram_tensor("x", [B, T, D], F32, kind="ExternalInput").ap()
    y_out = nc.dram_tensor("y", [B, T, D], F32, kind="ExternalOutput").ap()
    xmid = nc.dram_tensor("xmid", [B, T, D], F32, kind="Internal").ap()
    ssp = nc.dram_tensor("ssp", [B, T, D], BF16, kind="Internal").ap()
    xnsp = nc.dram_tensor("xnsp", [B, 128, NCH * T], BF16, kind="Internal").ap()
    if debug:
        DBGK = "ExternalOutput"
        dbg_xnt0 = nc.dram_tensor("dbg_xnt0", [B, 128, NCH * T], BF16, kind=DBGK).ap()
        dbg_s0 = nc.dram_tensor("dbg_s0", [B, T, D], BF16, kind=DBGK).ap()
        dbg_kt0 = nc.dram_tensor("dbg_kt0", [B, 128, NCH * T], BF16, kind=DBGK).ap()
        dbg_v0 = nc.dram_tensor("dbg_v0", [B, 128, TT * H * (DH + 1)], BF16, kind=DBGK).ap()
        dbg_qt0 = nc.dram_tensor("dbg_qt0", [B, 128, NCH * QB], BF16, kind=DBGK).ap()
        dbg_at0 = nc.dram_tensor("dbg_at0", [B, 128, H * QB], BF16, kind=DBGK).ap()
        dbg_g = nc.dram_tensor("dbg_g", [B, T, D], BF16, kind=DBGK).ap()
        dbg_h = nc.dram_tensor("dbg_h", [128, 256], BF16, kind=DBGK).ap()
        dbg_sg = nc.dram_tensor("dbg_sg", [128, 256], BF16, kind=DBGK).ap()
        dbg_xmid = nc.dram_tensor("dbg_xmid", [B, T, D], F32, kind=DBGK).ap()
        dbg_st = nc.dram_tensor("dbg_st", [128, 6 * TT], F32, kind=DBGK).ap()
        dbg_ub = nc.dram_tensor("dbg_ub", [128, 3 * D], BF16, kind=DBGK).ap()
        dbg_qsc = nc.dram_tensor("dbg_qsc", [128, D], F32, kind=DBGK).ap()
        dbg_rq = nc.dram_tensor("dbg_rq", [128, 4], F32, kind=DBGK).ap()

    wq = {}
    for l in (0, 1):
        for nm in ("qwT", "kwT", "vwT", "owT"):
            wq[(l, nm)] = nc.dram_tensor(f"l{l}_{nm}", [D, D], F32, kind="ExternalInput").ap()
        for nm in ("g_q", "g_k", "g_v", "b_q", "b_k", "b_v"):
            wq[(l, nm)] = nc.dram_tensor(f"l{l}_{nm}", [128, NCH], F32, kind="ExternalInput").ap()
        for nm in ("qb", "kb", "vb", "ob"):
            wq[(l, nm)] = nc.dram_tensor(f"l{l}_{nm}", [1, D], F32, kind="ExternalInput").ap()
        wq[(l, "beta")] = nc.dram_tensor(f"l{l}_beta", [1, H], F32, kind="ExternalInput").ap()
    wg_t = nc.dram_tensor("wg_t", [D, L0H], I32, kind="ExternalInput").ap()
    wu_t = nc.dram_tensor("wu_t", [D, L0H], I32, kind="ExternalInput").ap()
    wd_t = nc.dram_tensor("wd_t", [L0H, D], I32, kind="ExternalInput").ap()
    rwg = nc.dram_tensor("rwg", [D, NEXP], F32, kind="ExternalInput").ap()
    rbg = nc.dram_tensor("rbg", [1, NEXP], F32, kind="ExternalInput").ap()
    w1 = nc.dram_tensor("w1", [NEXP, D, L1H], F32, kind="ExternalInput").ap()
    b1 = nc.dram_tensor("b1", [128, NEXP * NH1], F32, kind="ExternalInput").ap()
    w2 = nc.dram_tensor("w2", [NEXP, L1H, D], F32, kind="ExternalInput").ap()
    b2 = nc.dram_tensor("b2", [1, NEXP * D], F32, kind="ExternalInput").ap()

    P = Prog(nc)
    ctxs = []

    def mksem(name):
        g = nc.semaphore(name)
        h = g.__enter__()
        ctxs.append(g)
        return Sem(h)

    SPE = mksem("spe")
    SACT = mksem("sact")
    SDVE = mksem("sdve")
    SGPC = mksem("sgpc")
    SSY = mksem("ssy")
    DW = mksem("dw")
    DX = mksem("dx")
    DS = mksem("ds")
    P.self_sems = {"vector": SDVE, "scalar": SACT}

    def barrier():
        def mkinc(S):
            def f(e):
                e.wait_ge(S.h, 0) if False else None
                return e.sem_inc(S.h, 1)
            return f
        P.sy(lambda e: e.sem_inc(SSY.h, 1), wait=[(DX, DX.n), (DS, DS.n)], inc=False)
        SSY.n += 1
        P.gp(lambda e: (e.drain(fusable=False), e.sem_inc(SGPC.h, 1))[1], wait=[(DW, DW.n)], inc=False)
        SGPC.n += 1
        P.te(lambda e: (e.drain(fusable=False), e.sem_inc(SPE.h, 1))[1], inc=False)
        SPE.n += 1
        P.sc(lambda e: (e.drain(fusable=False), e.sem_inc(SACT.h, 1))[1], inc=False)
        SACT.n += 1
        P.ve(lambda e: (e.drain(fusable=False), e.sem_inc(SDVE.h, 1))[1], inc=False)
        SDVE.n += 1
        sems = [SPE, SACT, SDVE, SGPC, SSY]
        for emitfn in (P.sy, P.gp, P.te, P.sc, P.ve):
            emitfn(lambda e: e.nop(), wait=list(sems), inc=False)

    def dxv():
        return (DX, DX.n)

    def dwv():
        return (DW, DW.n)

    psg = nc.psum_tensor("ps", [128, 4096], F32)
    ps = psg.__enter__()
    ctxs.append(psg)

    def bank(i, n=1):
        return ps[:, i * 512:(i + n) * 512]

    A = SBAlloc(nc, POOL_BYTES)
    ident = A.alloc("ident", [128, 128], BF16)
    ones_bf = A.alloc("ones_bf", [128, 128], BF16)
    ones32 = A.alloc("ones32", [128, 128], F32)
    eps_c = A.alloc("eps_c", [128, 1], F32)
    trash = A.alloc("trash", [128, 1024], BF16)
    st_scale = A.alloc("st_scale", [128, TT], F32)
    st_bias = A.alloc("st_bias", [128, TT], F32)
    st_r2 = A.alloc("st_r2", [128, TT], F32)
    st_tmp = A.alloc("st_tmp", [128, 4 * TT], F32)
    sums = A.alloc("sums", [128, TT], F32)
    sumsq = A.alloc("sumsq", [128, TT], F32)
    rk_rec = A.alloc("rk_rec", [128, TT], F32)
    beta8 = A.alloc("beta8", [128, H], F32)
    gam = A.alloc("gam", [128, 3 * NCH], F32)
    betv = A.alloc("betv", [128, 3 * NCH], F32)
    betv_bf = A.alloc("betv_bf", [128, 3 * NCH], BF16)
    w_moe = A.alloc("w_moe", [128, 2 * TT * NEXP], F32)
    b1_sb = A.alloc("b1_sb", [128, NEXP * NH1], F32)
    bgrow = A.alloc("bgrow", [1, NEXP], BF16)
    rwg_sb = A.alloc("rwg_sb", [128, NCH * NEXP], BF16)
    betarow = A.alloc("betarow", [1, H], F32)

    def emit_setup():
        io1 = trash[:].bitcast(F32)[:, 0:128]
        io2 = trash[:].bitcast(F32)[:, 128:129]
        P.gp(lambda e: e.iota(io1, [[1, 128]], channel_multiplier=0,
                              allow_small_or_imprecise_dtypes=True), inc=SGPC, inc_k=1)
        P.gp(lambda e: e.iota(io2, [[0, 1]], channel_multiplier=1,
                              allow_small_or_imprecise_dtypes=True), inc=SGPC, inc_k=1)
        P.ve(lambda e: e.memset(ones_bf[:], 1.0), inc=SDVE)
        P.ve(lambda e: e.memset(ones32[:], 1.0), inc=SDVE)
        P.ve(lambda e: e.memset(eps_c[:], EPS), inc=SDVE)
        P.ve(lambda e: e.tensor_scalar(ident[:], io1, io2, None, ALU.is_equal),
             wait=[SGPC], inc=SDVE)
        P.sy(lambda e: e.dma_start(out=b1_sb[:], in_=b1), inc=DX)
        P.gp(lambda e: e.dma_start(out=bgrow[:], in_=rbg), inc=DW)
        P.gp(lambda e: e.dma_start(
            out=rwg_sb[:].rearrange("p (c n) -> p c n", n=NEXP),
            in_=rwg.rearrange("(c p) n -> p c n", p=128)), inc=DW)
        barrier()

    def load_layer_params(l):
        for j, nm in enumerate(("g_q", "g_k", "g_v")):
            P.sy(lambda e, nm=nm, j=j: e.dma_start(
                out=gam[:, j * NCH:(j + 1) * NCH], in_=wq[(l, nm)]), inc=DX)
        for j, nm in enumerate(("b_q", "b_k", "b_v")):
            P.sy(lambda e, nm=nm, j=j: e.dma_start(
                out=betv[:, j * NCH:(j + 1) * NCH], in_=wq[(l, nm)]), inc=DX)
        P.sy(lambda e: e.dma_start(out=betarow[:], in_=wq[(l, "beta")]), inc=DX)
        barrier()
        P.ve(lambda e: e.tensor_copy(betv_bf[:], betv[:]), inc=SDVE)
        P.sc(lambda e: e.activation(betarow[:], betarow[:], AF.Exp), inc=SACT)
        P.sc(lambda e: e.activation(betarow[:], betarow[:], AF.Ln, bias=1.0), wait=[SACT], inc=SACT)
        P.ve(lambda e: e.tensor_scalar(betarow[:], betarow[:], 10.0, 0.1, ALU.min, ALU.max),
             wait=[SACT], inc=SDVE)
        P.ve(lambda e: e.tensor_scalar_mul(betarow[:], betarow[:], 0.125), inc=SDVE)
        P.te(lambda e: e.matmul(ps[:, 0:H], ones32[0:1, :], betarow[:],
                                start=True, stop=True), wait=[SDVE], inc=SPE)
        P.ve(lambda e: e.tensor_copy(beta8[:], ps[:, 0:H]), wait=[SPE], inc=SDVE)
        barrier()

    def emit_u(j, Wraw, brow_seg, ubrow, ubrow_bf, ub_rep):
        for jh in range(2):
            for c in range(NCH):
                P.te(lambda e, jh=jh, c=c, W=Wraw, j=j: e.matmul(
                    ps[0:1, 512 + jh * 512:512 + (jh + 1) * 512] if False else ps[0:1, jh * 512:(jh + 1) * 512],
                    betv_bf[:, j * NCH + c:j * NCH + c + 1],
                    W[:, c * D + jh * 512:c * D + (jh + 1) * 512],
                    start=(c == 0), stop=(c == NCH - 1)),
                    inc=SPE if c == NCH - 1 else None)
            P.ve(lambda e, jh=jh: e.tensor_copy(
                ubrow[:, jh * 512:(jh + 1) * 512], ps[0:1, jh * 512:(jh + 1) * 512]),
                wait=[SPE], inc=SDVE)
        P.ve(lambda e: e.tensor_add(ubrow[:], ubrow[:], brow_seg), wait=[dxv()], inc=SDVE)
        P.ve(lambda e: e.tensor_copy(ubrow_bf[:], ubrow[:]), inc=SDVE)
        for jh in range(2):
            P.te(lambda e, jh=jh: e.matmul(
                bank(2 + jh), ones_bf[0:1, :], ubrow_bf[:, jh * 512:(jh + 1) * 512],
                start=True, stop=True), wait=[SDVE], inc=SPE)
        P.ve(lambda e, j=j: e.tensor_copy(ub_rep[:, j * D:(j + 1) * D], bank(2, 2)),
             wait=[SPE], inc=SDVE)

    def emit_phaseA(l, s, xnT, sw, dbgA=False):
        xsrc = x_in if l == 0 else xmid
        kT, v_sb, ub_rep, betak = sw["kT"], sw["v_sb"], sw["ub_rep"], sw["betak"]

        A.push()
        kW = A.alloc("kW", [128, NCH * D], BF16)
        vW = A.alloc("vW", [128, NCH * D], BF16)
        xt = [A.alloc("xt", [128, D], F32) for _ in range(2)]
        xnsc = A.alloc("xnsc", [128, D], BF16)
        ksc = A.alloc("ksc", [128, D], F32)
        ksc_bf = A.alloc("ksc_bf", [128, D], BF16)
        brow = A.alloc("brow", [1, 2 * D], F32)
        ubrow = A.alloc("ubrow", [1, D], F32)
        ubrow_bf = A.alloc("ubrow_bf", [1, D], BF16)

        def wload(dst, src):
            P.gp(lambda e: e.dma_start(
                out=dst[:].rearrange("p (c n) -> p c n", n=D),
                in_=src.rearrange("(c p) n -> p c n", p=128)), inc=DW)
        wload(kW, wq[(l, "kwT")])
        wload(vW, wq[(l, "vwT")])
        wl_done = (DW, DW.n)
        for j, nm in enumerate(("kb", "vb")):
            P.sy(lambda e, nm=nm, j=j: e.dma_start(
                out=brow[:, j * D:(j + 1) * D], in_=wq[(l, nm)]), inc=DX)

        for tt in range(TT):
            buf = xt[tt % 2]
            P.sy(lambda e, b=buf, tt=tt, s=s: e.dma_start(
                out=b[:], in_=xsrc[s, tt * 128:(tt + 1) * 128, :]),
                wait=[SACT, SDVE] if tt >= 2 else [], inc=DX)
            P.ve(lambda e, b=buf, tt=tt: e.reduce_sum(sums[:, tt:tt + 1], b[:], axis=AX.X),
                 wait=[dxv()], inc=SDVE)
            P.sc(lambda e, b=buf, tt=tt: e.activation(
                trash[:], b[:], AF.Square, accum_out=sumsq[:, tt:tt + 1]),
                wait=[dxv()], inc=SACT)
        barrier()

        mean = st_tmp[:, 0:TT]
        var = st_tmp[:, TT:2 * TT]
        t3 = st_tmp[:, 2 * TT:3 * TT]
        P.ve(lambda e: e.tensor_scalar_mul(mean, sums[:], 1.0 / D), inc=SDVE)
        P.sc(lambda e: e.activation(var, sumsq[:], AF.Copy, scale=1.0 / D), inc=SACT)
        P.ve(lambda e: e.tensor_mul(t3, mean, mean), inc=SDVE)
        P.ve(lambda e: e.tensor_sub(var, var, t3), wait=[SACT], inc=SDVE)
        P.sc(lambda e: e.activation(t3, var, AF.Sqrt, bias=eps_c[:, 0:1]), wait=[SDVE], inc=SACT)
        P.ve(lambda e: e.reciprocal(st_scale[:], t3), wait=[SACT], inc=SDVE)
        P.ve(lambda e: e.scalar_tensor_tensor(st_bias[:], mean, -1.0, st_scale[:],
                                              ALU.mult, ALU.mult), inc=SDVE)
        P.ve(lambda e: e.tensor_mul(t3, st_scale[:], st_scale[:]), inc=SDVE)
        P.ve(lambda e: e.tensor_mul(t3, t3, var), inc=SDVE)
        P.sc(lambda e: e.activation(t3, t3, AF.Sqrt, bias=eps_c[:, 0:1]), wait=[SDVE], inc=SACT)
        P.ve(lambda e: e.reciprocal(st_r2[:], t3), wait=[SACT], inc=SDVE)
        barrier()
        if dbgA and l == 0 and s == 0:
            def _dumpst(e):
                return e.dma_start(out=dbg_st, in_=st_tmp[:, 0:0] if False else None)
            P.sy(lambda e: e.dma_start(out=dbg_st[:, 0:TT], in_=sums[:]), inc=DS)
            P.sy(lambda e: e.dma_start(out=dbg_st[:, TT:2 * TT], in_=sumsq[:]), inc=DS)
            P.sy(lambda e: e.dma_start(out=dbg_st[:, 2 * TT:3 * TT], in_=st_scale[:]), inc=DS)
            P.sy(lambda e: e.dma_start(out=dbg_st[:, 3 * TT:4 * TT], in_=st_bias[:]), inc=DS)
            P.sy(lambda e: e.dma_start(out=dbg_st[:, 4 * TT:5 * TT], in_=st_r2[:]), inc=DS)
            barrier()

        emit_u(1, kW, brow[:, 0:D], ubrow, ubrow_bf, ub_rep)
        emit_u(2, vW, brow[:, D:2 * D], ubrow, ubrow_bf, ub_rep)
        barrier()
        for j, Wt in ((1, kW), (2, vW)):
            for c in range(NCH):
                P.ve(lambda e, j=j, W=Wt, c=c: e.tensor_scalar_mul(
                    W[:, c * D:(c + 1) * D], W[:, c * D:(c + 1) * D],
                    gam[:, j * NCH + c:j * NCH + c + 1]), inc=SDVE)
        P.ve(lambda e: e.memset(
            v_sb[:].rearrange("p (t h c) -> p t h c", h=H, c=DH + 1)[:, :, :, DH:DH + 1],
            1.0), inc=SDVE)
        barrier()

        for half in range(2):
            for i in range(8):
                tt = half * 8 + i
                buf = xt[tt % 2]
                P.sy(lambda e, b=buf, tt=tt, s=s: e.dma_start(
                    out=b[:], in_=xsrc[s, tt * 128:(tt + 1) * 128, :]),
                    wait=[SDVE] if i >= 2 else [], inc=DX)
                P.ve(lambda e, b=buf, tt=tt: e.tensor_scalar(
                    xnsc[:], b[:], st_scale[:, tt:tt + 1], st_bias[:, tt:tt + 1],
                    ALU.mult, ALU.add),
                    wait=[dxv(), SPE], inc=SDVE)
                for c in range(NCH):
                    P.te(lambda e, c=c, i=i: e.transpose(
                        bank(c).bitcast(BF16)[:, i * 128:(i + 1) * 128],
                        xnsc[:, c * 128:(c + 1) * 128], ident[:]),
                        wait=[SDVE], inc=SPE if c == NCH - 1 else None)
            for c in range(NCH):
                P.ve(lambda e, c=c, half=half: e.tensor_copy(
                    xnT[:].rearrange("p (c t) -> p c t", t=T)[:, c, half * 1024:(half + 1) * 1024],
                    bank(c).bitcast(BF16)), wait=[SPE], inc=SDVE)
            barrier()

        for tt in range(TT):
            for c in range(NCH):
                for (W, bk) in ((kW, 0), (vW, 2)):
                    for jh in range(2):
                        P.te(lambda e, c=c, tt=tt, W=W, bk=bk, jh=jh: e.matmul(
                            bank(bk + jh),
                            xnT[:, c * T + tt * 128:c * T + (tt + 1) * 128],
                            W[:, c * D + jh * 512:c * D + (jh + 1) * 512],
                            start=(c == 0), stop=(c == NCH - 1)),
                            wait=[SDVE] if (c == 0 and bk == 0 and jh == 0) else [],
                            inc=SPE if (c == NCH - 1 and bk == 2 and jh == 1) else None)
            for jh in range(2):
                P.ve(lambda e, jh=jh, tt=tt: e.scalar_tensor_tensor(
                    ksc[:, jh * 512:(jh + 1) * 512], bank(jh),
                    st_r2[:, tt:tt + 1],
                    ub_rep[:, D + jh * 512:D + (jh + 1) * 512],
                    ALU.mult, ALU.add), wait=[SPE], inc=SDVE)
                P.ve(lambda e, jh=jh, tt=tt: e.scalar_tensor_tensor(
                    v_sb[:].rearrange("p (t h c) -> p t h c", h=H, c=DH + 1)
                    [:, tt, jh * 8:(jh + 1) * 8, 0:DH],
                    bank(2 + jh).rearrange("p (h c) -> p h c", c=DH),
                    st_r2[:, tt:tt + 1],
                    ub_rep[:, 2 * D + jh * 512:2 * D + (jh + 1) * 512].rearrange(
                        "o (h c) -> o h c", c=DH),
                    ALU.mult, ALU.add), inc=SDVE)
            P.sc(lambda e, tt=tt: e.activation(trash[:], ksc[:], AF.Square,
                                               accum_out=rk_rec[:, tt:tt + 1]),
                 wait=[SDVE], inc=SACT)
            P.sc(lambda e, tt=tt: e.activation(rk_rec[:, tt:tt + 1], rk_rec[:, tt:tt + 1],
                                               AF.Sqrt), wait=[SACT], inc=SACT)
            P.ve(lambda e, tt=tt: e.tensor_scalar_max(rk_rec[:, tt:tt + 1], rk_rec[:, tt:tt + 1],
                                                      EPSN), wait=[SACT], inc=SDVE)
            P.ve(lambda e, tt=tt: e.reciprocal(rk_rec[:, tt:tt + 1], rk_rec[:, tt:tt + 1]),
                 inc=SDVE)
            P.ve(lambda e, tt=tt: e.tensor_scalar_mul(ksc_bf[:], ksc[:], rk_rec[:, tt:tt + 1]),
                 inc=SDVE)
            for c in range(NCH):
                P.te(lambda e, c=c, tt=tt: e.transpose(
                    bank(4 + tt % 2).bitcast(BF16)[:, c * 128:(c + 1) * 128],
                    ksc_bf[:, c * 128:(c + 1) * 128], ident[:]),
                    wait=[SDVE], inc=SPE if c == NCH - 1 else None)
            P.ve(lambda e, tt=tt: e.tensor_copy(
                kT[:].rearrange("p (c t) -> p c t", t=T)[:, :, tt * 128:(tt + 1) * 128],
                bank(4 + tt % 2).bitcast(BF16).rearrange("p (c t) -> p c t", t=128)),
                wait=[SPE], inc=SDVE)
        barrier()

        if l == 1:
            pcol = st_tmp[:, 3 * TT:3 * TT + NEXP]
            mcol = st_tmp[:, 3 * TT + NEXP:3 * TT + 2 * NEXP]
            for tt in range(TT):
                for c in range(NCH):
                    P.te(lambda e, c=c, tt=tt: e.matmul(
                        ps[:, 0:NEXP],
                        xnT[:, c * T + tt * 128:c * T + (tt + 1) * 128],
                        rwg_sb[:, c * NEXP:(c + 1) * NEXP],
                        start=(c == 0), stop=False),
                        wait=[SACT] if c == 0 else [], inc=None)
                P.te(lambda e: e.matmul(ps[:, 0:NEXP], ones_bf[0:1, :], bgrow[:],
                                        start=False, stop=True), inc=SPE)
                P.sc(lambda e, pcol=pcol: e.activation(pcol, ps[:, 0:NEXP], AF.Sigmoid),
                     wait=[SPE], inc=SACT)
                P.ve(lambda e, pcol=pcol, mcol=mcol: e.tensor_scalar(
                    mcol, pcol, 0.5, None, ALU.is_gt), wait=[SACT], inc=SDVE)
                P.ve(lambda e, s=s, tt=tt, pcol=pcol, mcol=mcol: e.tensor_tensor(
                    w_moe[:, (s * TT + tt) * NEXP:(s * TT + tt + 1) * NEXP],
                    pcol, mcol, ALU.mult), inc=SDVE)
        barrier()
        A.pop()

    def emit_attn(l, s, xnT, sw, dbg=None):
        xsrc = x_in if l == 0 else xmid
        kT, v_sb, ub_rep, betak = sw["kT"], sw["v_sb"], sw["ub_rep"], sw["betak"]
        A.push()
        qW = A.alloc("qW", [128, NCH * D], BF16)
        oW = A.alloc("oW", [64, H * D], BF16)
        qTb = A.alloc("qTb", [128, NCH * QB], BF16)
        E_sb = A.alloc("E_sb", [128, 4096], BF16)
        attnT = A.alloc("attnT", [128, H * QB], BF16)
        rs_rec = A.alloc("rs_rec", [128, 512], F32)
        obrow = A.alloc("obrow", [1, D], BF16)

        A.push()
        brow_q = A.alloc("brow_q", [1, D], F32)
        ubrow = A.alloc("ubrow_a", [1, D], F32)
        ubrow_bf = A.alloc("ubrow_abf", [1, D], BF16)

        P.gp(lambda e: e.dma_start(
            out=qW[:].rearrange("p (c n) -> p c n", n=D),
            in_=wq[(l, "qwT")].rearrange("(c p) n -> p c n", p=128)), inc=DW)
        P.gp(lambda e: e.dma_start(
            out=oW[:].rearrange("p (h n) -> p h n", n=D),
            in_=wq[(l, "owT")].rearrange("(h p) n -> p h n", p=64)), inc=DW)
        P.gp(lambda e: e.dma_start(out=obrow[:], in_=wq[(l, "ob")]), inc=DW)
        P.sy(lambda e: e.dma_start(out=brow_q[:], in_=wq[(l, "qb")]), inc=DX)
        P.ve(lambda e: e.memset(rs_rec[:], 0.0), inc=SDVE)
        barrier()
        emit_u(0, qW, brow_q[:], ubrow, ubrow_bf, ub_rep)
        barrier()
        for c in range(NCH):
            P.ve(lambda e, c=c: e.tensor_scalar_mul(
                qW[:, c * D:(c + 1) * D], qW[:, c * D:(c + 1) * D],
                gam[:, c:c + 1]), inc=SDVE)
        barrier()
        if dbg is not None and l == 0 and s == 0:
            P.sy(lambda e, ub=ub_rep: e.dma_start(out=dbg_ub, in_=ub[:]), inc=DS)
            barrier()
        A.pop()

        if dbg is not None:
            P.ve(lambda e: e.memset(qTb[:], 0.0), inc=SDVE)
            P.ve(lambda e: e.memset(attnT[:], 0.0), inc=SDVE)
            barrier()
        qsc = A.alloc("qsc", [128, D], F32)
        qsc_bf = A.alloc("qsc_bf", [128, D], BF16)
        rqss = A.alloc("rqss", [128, 4], F32)
        xe = [A.alloc("xe", [128, D], F32)]
        stile = A.alloc("stile2", [128, D], BF16)

        pv_bank_free = [0, 0]
        bc_mark = [0, 0]
        norm_mark = [0, 0]
        epi_mark = [0]

        def emit_oproj_tile(oqb, i):
            tt = oqb * 4 + i
            P.sy(lambda e, i=i, tt=tt, s=s: e.dma_start(
                out=xe[0][:], in_=xsrc[s, tt * 128:(tt + 1) * 128, :]),
                wait=[(SDVE, epi_mark[0])], inc=DX)
            for jh in range(2):
                bk = 6 + jh
                for h in range(H):
                    wl = [(SDVE, epi_mark[0])] if (h == 0 and jh == 0) else []
                    P.te(lambda e, h=h, i=i, jh=jh, bk=bk, oqb=oqb: e.matmul(
                        bank(bk), attnT[0:64, h * QB + i * 128:h * QB + (i + 1) * 128],
                        oW[0:64, h * D + jh * 512:h * D + (jh + 1) * 512],
                        start=(h == 0), stop=False), wait=wl, inc=None)
                P.te(lambda e, jh=jh, bk=bk: e.matmul(
                    bank(bk), ones_bf[0:1, :], obrow[:, jh * 512:(jh + 1) * 512],
                    start=False, stop=True), inc=SPE)
            P.ve(lambda e, tt=tt: e.tensor_scalar(
                xe[0][:], xe[0][:], st_scale[:, tt:tt + 1], st_bias[:, tt:tt + 1],
                ALU.mult, ALU.add), wait=[dxv(), SPE], inc=SDVE)
            for jh in range(2):
                P.ve(lambda e, jh=jh: e.tensor_add(
                    stile[:, jh * 512:(jh + 1) * 512], bank(6 + jh),
                    xe[0][:, jh * 512:(jh + 1) * 512]),
                    wait=[(DS, DS.n)] if jh == 0 else [], inc=SDVE)
            epi_mark[0] = SDVE.n
            P.sy(lambda e, tt=tt, s=s: e.dma_start(
                out=ssp[s, tt * 128:(tt + 1) * 128, :], in_=stile[:]),
                wait=[SDVE], inc=DS)
        for qb in range(NQB):
            for i in range(4):
                tt = qb * 4 + i
                for c in range(NCH):
                    for jh in range(2):
                        P.te(lambda e, c=c, tt=tt, jh=jh: e.matmul(
                            bank(jh), xnT[:, c * T + tt * 128:c * T + (tt + 1) * 128],
                            qW[:, c * D + jh * 512:c * D + (jh + 1) * 512],
                            start=(c == 0), stop=(c == NCH - 1)),
                            wait=[SDVE] if (c == 0 and jh == 0) else [],
                            inc=SPE if (c == NCH - 1 and jh == 1) else None)
                for jh in range(2):
                    P.ve(lambda e, jh=jh, tt=tt: e.scalar_tensor_tensor(
                        qsc[:, jh * 512:(jh + 1) * 512], bank(jh),
                        st_r2[:, tt:tt + 1], ub_rep[:, jh * 512:(jh + 1) * 512],
                        ALU.mult, ALU.add), wait=[SPE], inc=SDVE)
                P.sc(lambda e, i=i: e.activation(trash[:], qsc[:], AF.Square,
                                                 accum_out=rqss[:, i:i + 1]),
                     wait=[SDVE], inc=SACT)
                P.sc(lambda e, i=i: e.activation(rqss[:, i:i + 1], rqss[:, i:i + 1],
                                                 AF.Sqrt), wait=[SACT], inc=SACT)
                P.ve(lambda e, i=i: e.tensor_scalar_max(rqss[:, i:i + 1], rqss[:, i:i + 1],
                                                        EPSN), wait=[SACT], inc=SDVE)
                P.ve(lambda e, i=i: e.reciprocal(rqss[:, i:i + 1], rqss[:, i:i + 1]),
                     inc=SDVE)
                P.ve(lambda e, i=i: e.tensor_scalar_mul(qsc_bf[:], qsc[:], rqss[:, i:i + 1]),
                     inc=SDVE)
                if dbg is not None and l == 0 and s == 0 and qb == 0 and i == 0:
                    P.sy(lambda e: e.dma_start(out=dbg_qsc, in_=qsc[:]), wait=[SDVE], inc=DS)
                    P.sy(lambda e: e.dma_start(out=dbg_rq[:, 0:1], in_=rqss[:, 0:1], allow_slow_non_contiguous=True), wait=[SDVE], inc=DS)
                    barrier()
                for c in range(NCH):
                    P.te(lambda e, c=c, i=i: e.transpose(
                        bank(2 + i % 2).bitcast(BF16)[:, c * 128:(c + 1) * 128],
                        qsc_bf[:, c * 128:(c + 1) * 128], ident[:]),
                        wait=[SDVE], inc=SPE if c == NCH - 1 else None)
                P.ve(lambda e, i=i: e.tensor_copy(
                    qTb[:].rearrange("p (c t) -> p c t", t=QB)[:, :, i * 128:(i + 1) * 128],
                    bank(2 + i % 2).bitcast(BF16).rearrange("p (c t) -> p c t", t=128)),
                    wait=[SPE], inc=SDVE)
                if qb >= 1:
                    emit_oproj_tile(qb - 1, i)
            barrier()
            if dbg is not None and qb == 0:
                P.sy(lambda e, s=s: e.dma_start(out=dbg[0][s], in_=qTb[:]), inc=DS)
                barrier()

            pend = []
            for pr in range(H // 2):
                c = pr
                s_marks = [[], []]
                exp_marks = [[], []]
                pv_marks = [[], []]

                def emit_S_k2(hh, ktg, k2):
                    rb = hh * 64
                    sbp = hh * 2
                    kt = 2 * ktg + k2
                    wl = []
                    if ktg >= 1 and k2 == 0:
                        wl.append((SACT, exp_marks[hh][ktg - 1]))
                    P.te(lambda e, c=c, rb=rb, sbp=sbp, kt=kt, k2=k2: e.matmul(
                        ps[:, (sbp + k2) * 512:(sbp + k2 + 1) * 512],
                        kT[rb:rb + 64, c * T + kt * 128:c * T + (kt + 1) * 128],
                        qTb[rb:rb + 64, c * QB:(c + 1) * QB],
                        start=True, stop=True, tile_position=(rb, 0)),
                        wait=wl, inc=SPE)

                def emit_exp(hh, ktg):
                    h = 2 * pr + hh
                    sbp = hh * 2
                    wl = [(SPE, s_marks[hh][ktg])]
                    if ktg >= 2:
                        wl.append((SPE, pv_marks[hh][ktg - 2]))
                    P.sc(lambda e, sbp=sbp, h=h, hh=hh, ktg=ktg: e.activation(
                        E_sb[:, hh * 2048 + (ktg % 2) * 1024:hh * 2048 + (ktg % 2) * 1024 + 1024],
                        ps[:, sbp * 512:(sbp + 2) * 512], AF.Exp,
                        scale=beta8[:, h:h + 1]),
                        wait=wl, self_wait=False)
                    exp_marks[hh].append(SACT.n)

                def emit_PV(hh, ktg):
                    h = 2 * pr + hh
                    for k2 in range(2):
                        kt = 2 * ktg + k2
                        wl = [(SACT, exp_marks[hh][ktg])]
                        if kt == 0:
                            wl.append((SDVE, pv_bank_free[hh]))
                        P.te(lambda e, kt=kt, h=h, hh=hh, ktg=ktg, k2=k2: e.matmul(
                            ps[0:65, (4 + hh) * 512:(5 + hh) * 512],
                            v_sb[:].rearrange("p (t h c) -> p t h c", h=H, c=DH + 1)[:, kt, h, :],
                            E_sb[:, hh * 2048 + (ktg % 2) * 1024 + k2 * 512:
                                 hh * 2048 + (ktg % 2) * 1024 + (k2 + 1) * 512],
                            start=(kt == 0), stop=(kt == TT - 1)),
                            wait=wl, inc=SPE if k2 == 1 else None)
                    pv_marks[hh].append(SPE.n)

                for ktg in range(8):
                    emit_S_k2(0, ktg, 0)
                    emit_S_k2(1, ktg, 0)
                    emit_S_k2(0, ktg, 1)
                    emit_S_k2(1, ktg, 1)
                    for hh in range(2):
                        s_marks[hh].append(SPE.n)
                    for hh in range(2):
                        emit_exp(hh, ktg)
                    if ktg >= 1:
                        for hh in range(2):
                            emit_PV(hh, ktg - 1)
                    if ktg in (3, 4) and pend:
                        pend.pop(0)()
                for hh in range(2):
                    emit_PV(hh, 7)
                for hh in range(2):
                    h = 2 * pr + hh
                    P.ve(lambda e, h=h, hh=hh: e.tensor_copy(
                        attnT[0:64, h * QB:(h + 1) * QB],
                        ps[0:64, (4 + hh) * 512:(5 + hh) * 512]), wait=[SPE], inc=SDVE)
                    rslot = rs_rec[64:65, 0:512] if hh == 0 else trash[:].bitcast(F32)[64:65, 0:512]
                    P.ve(lambda e, hh=hh, rslot=rslot: e.reciprocal(
                        rslot, ps[64:65, (4 + hh) * 512:(5 + hh) * 512]),
                        wait=[(SPE, bc_mark[hh])], inc=SDVE)
                    pv_bank_free[hh] = SDVE.n
                    rc_mark = SDVE.n

                    def fin(h=h, hh=hh, rc_mark=rc_mark, rslot=rslot):
                        P.te(lambda e, hh=hh, rslot=rslot: e.matmul(
                            bank(6 + hh), ones32[64:65, :], rslot,
                            start=True, stop=True),
                            wait=[(SDVE, rc_mark), (SDVE, norm_mark[hh])], inc=SPE)
                        bc_mark[hh] = SPE.n
                        P.ve(lambda e, h=h, hh=hh: e.tensor_tensor(
                            attnT[0:64, h * QB:(h + 1) * QB],
                            attnT[0:64, h * QB:(h + 1) * QB],
                            bank(6 + hh)[0:64, :], ALU.mult),
                            wait=[(SPE, bc_mark[hh])], inc=SDVE)
                        norm_mark[hh] = SDVE.n
                    pend.append(fin)
            while pend:
                pend.pop(0)()
            if qb == NQB - 1:
                for i in range(4):
                    emit_oproj_tile(qb, i)
            barrier()
        if l == 1:
            P.sy(lambda e, s=s: e.dma_start(out=xnsp[s], in_=xnT[:]), inc=DS)
            barrier()
        A.pop()

    def emit_ffn0(s, xnT, dbg=False):
        A.push()
        wgS = A.alloc("wgS", [128, NCH * L0H], BF16)
        wuS = A.alloc("wuS", [128, NCH * L0H], BF16)
        wdS = A.alloc("wdS", [128, NH0 * D], BF16)
        sgb = [A.alloc("sgb", [128, 256], BF16) for _ in range(2)]
        hb = [A.alloc("hb", [128, 256], BF16) for _ in range(2)]
        gxe = A.alloc("gxe", [128, D], F32)
        gse = A.alloc("gse", [128, D], BF16)
        ggb = A.alloc("ggb", [128, D], BF16)
        gob = A.alloc("gob", [128, D], F32)

        P.gp(lambda e: e.dma_start(
            out=wgS[:].rearrange("p (c n) -> p c n", n=L0H),
            in_=wg_t.rearrange("(c p) n -> p c n", p=128)), inc=DW)
        P.gp(lambda e: e.dma_start(
            out=wuS[:].rearrange("p (c n) -> p c n", n=L0H),
            in_=wu_t.rearrange("(c p) n -> p c n", p=128)), inc=DW)
        P.gp(lambda e: e.dma_start(
            out=wdS[:].rearrange("p (c n) -> p c n", n=D),
            in_=wd_t.rearrange("(c p) n -> p c n", p=128)), inc=DW)
        barrier()

        for tq in range(T // 256):
            stt_mark = {}
            silu_mark = {}
            down_mark = {}

            def emit_gu(ht):
                gb, ub2 = ht % 2, 2 + ht % 2
                for c in range(NCH):
                    P.te(lambda e, c=c, ht=ht, tq=tq, gb=gb: e.matmul(
                        ps[:, gb * 512:gb * 512 + 256],
                        wgS[:, c * L0H + ht * 128:c * L0H + (ht + 1) * 128],
                        xnT[:].rearrange("p (c t) -> p c t", t=T)[:, c, tq * 256:(tq + 1) * 256],
                        start=(c == 0), stop=(c == NCH - 1)),
                        wait=[(SACT, silu_mark[ht - 2])] if (ht >= 2 and c == 0) else [],
                        inc=None)
                for c in range(NCH):
                    P.te(lambda e, c=c, ht=ht, tq=tq, ub2=ub2: e.matmul(
                        ps[:, ub2 * 512:ub2 * 512 + 256],
                        wuS[:, c * L0H + ht * 128:c * L0H + (ht + 1) * 128],
                        xnT[:].rearrange("p (c t) -> p c t", t=T)[:, c, tq * 256:(tq + 1) * 256],
                        start=(c == 0), stop=(c == NCH - 1)),
                        wait=[(SDVE, stt_mark[ht - 2])] if (ht >= 2 and c == 0) else [],
                        inc=SPE if c == NCH - 1 else None)
                P.sc(lambda e, gb=gb: e.activation(
                    sgb[ht % 2][:], ps[:, gb * 512:gb * 512 + 256], AF.Silu, scale=1.0 / 127.0),
                    wait=[SPE, (SDVE, stt_mark.get(ht - 2, 0))], self_wait=False)
                silu_mark[ht] = SACT.n
                P.ve(lambda e, ub2=ub2, ht=ht: e.scalar_tensor_tensor(
                    hb[ht % 2][:], ps[:, ub2 * 512:ub2 * 512 + 256], 1.0 / 127.0,
                    sgb[ht % 2][:], ALU.mult, ALU.mult),
                    wait=[(SACT, silu_mark[ht]), (SPE, down_mark.get(ht - 2, 0))], inc=SDVE)
                stt_mark[ht] = SDVE.n

            def emit_down(ht):
                for tsub in range(2):
                    for jh in range(2):
                        P.te(lambda e, ht=ht, tsub=tsub, jh=jh: e.matmul(
                            bank(4 + tsub * 2 + jh),
                            hb[ht % 2][:, tsub * 128:(tsub + 1) * 128],
                            wdS[:, ht * D + jh * 512:ht * D + (jh + 1) * 512],
                            start=(ht == 0), stop=(ht == NH0 - 1)),
                            wait=[(SDVE, stt_mark[ht])] if (tsub == 0 and jh == 0) else [],
                            inc=SPE if (tsub == 1 and jh == 1) else None)
                down_mark[ht] = SPE.n

            for ht in range(NH0):
                emit_gu(ht)
                if ht >= 1:
                    emit_down(ht - 1)
            emit_down(NH0 - 1)
            for tsub in range(2):
                tt = tq * 2 + tsub
                P.sy(lambda e, s=s, tt=tt: e.dma_start(
                    out=gxe[:], in_=x_in[s, tt * 128:(tt + 1) * 128, :]),
                    wait=[SDVE] if tsub == 1 else [], inc=DX)
                P.sy(lambda e, s=s, tt=tt: e.dma_start(
                    out=gse[:], in_=ssp[s, tt * 128:(tt + 1) * 128, :]), inc=DX)
                for jh in range(2):
                    P.sc(lambda e, tsub=tsub, jh=jh: e.activation(
                        ggb[:, jh * 512:(jh + 1) * 512], bank(4 + tsub * 2 + jh),
                        AF.Silu, scale=1.0 / 127.0), wait=[SPE, SDVE], inc=SACT)
                if dbg:
                    P.sy(lambda e, s=s, tt=tt: e.dma_start(
                        out=dbg_g[s, tt * 128:(tt + 1) * 128, :], in_=ggb[:]),
                        wait=[SACT], inc=DS)
                    barrier()
                P.ve(lambda e: e.tensor_mul(ggb[:], gse[:], ggb[:]),
                     wait=[SACT, dxv()], inc=SDVE)
                P.ve(lambda e: e.tensor_add(gob[:], gxe[:], ggb[:]),
                     wait=[(DS, DS.n)], inc=SDVE)
                P.sy(lambda e, s=s, tt=tt: e.dma_start(
                    out=xmid[s, tt * 128:(tt + 1) * 128, :], in_=gob[:]),
                    wait=[SDVE], inc=DS)
            barrier()
        A.pop()

    def emit_ffn1():
        A.push()
        w1S = A.alloc("w1S", [128, NCH * L1H], BF16)
        w2S = A.alloc("w2S", [128, NH1 * D], BF16)
        acc = A.alloc("acc", [128, 8 * D], F32)
        xnh = A.alloc("xnh", [128, NCH * 1024], BF16)
        b2e = A.alloc("b2e", [1, D], BF16)
        hb = [A.alloc("hbm", [128, 256], BF16) for _ in range(2)]
        gxe = A.alloc("gxe1", [128, D], F32)
        gse = A.alloc("gse1", [128, D], BF16)
        ggb = A.alloc("ggb1", [128, D], BF16)
        gob = A.alloc("gob1", [128, D], F32)

        w1c_reader = [0] * NCH
        w2g_reader = [0] * 8
        b2_reader = [0]
        for sw_i in range(4):
            s, half = sw_i // 2, sw_i % 2
            P.sy(lambda e, s=s, half=half: e.dma_start(
                out=xnh[:].rearrange("p (c t) -> p c t", t=1024),
                in_=xnsp[s].rearrange("p (c t) -> p c t", t=T)[:, :, half * 1024:(half + 1) * 1024]),
                inc=DX)
            barrier()
            acc_mark = [0] * 8
            last_acc = [0]
            for e_ in range(NEXP):
                w1c_mark = []
                for cc in range(NCH):
                    P.gp(lambda e, e_=e_, cc=cc: e.dma_start(
                        out=w1S[:, cc * L1H:(cc + 1) * L1H],
                        in_=w1[e_, cc * 128:(cc + 1) * 128, :]),
                        wait=[(SPE, w1c_reader[cc])], inc=DW)
                    w1c_mark.append(DW.n)
                w2g_mark = []
                for gg in range(8):
                    P.gp(lambda e, e_=e_, gg=gg: e.dma_start(
                        out=w2S[:, gg * 4096:(gg + 1) * 4096],
                        in_=w2[e_, gg * 512:(gg + 1) * 512, :]),
                        wait=[(SPE, w2g_reader[gg])], inc=DW)
                    w2g_mark.append(DW.n)
                P.gp(lambda e, e_=e_: e.dma_start(
                    out=b2e[:], in_=b2[0:1, e_ * D:(e_ + 1) * D]),
                    wait=[(SPE, b2_reader[0])], inc=DW)
                b2_mark = DW.n
                for tq in range(4):
                    relu_mark = {}

                    def emit_w1(ht):
                        pb = ht % 2
                        for c in range(NCH):
                            wl = []
                            if c == 0 and ht >= 2:
                                wl.append((SACT, relu_mark[ht - 2]))
                            if tq == 0:
                                wl.append((DW, w1c_mark[c]))
                            P.te(lambda e, c=c, ht=ht, tq=tq, pb=pb: e.matmul(
                                ps[:, pb * 512:pb * 512 + 256],
                                w1S[:, c * L1H + ht * 128:c * L1H + (ht + 1) * 128],
                                xnh[:, c * 1024 + tq * 256:c * 1024 + (tq + 1) * 256],
                                start=(c == 0), stop=(c == NCH - 1)),
                                wait=wl, inc=SPE if c == NCH - 1 else None)
                        P.sc(lambda e, pb=pb, ht=ht, e_=e_: e.activation(
                            hb[ht % 2][:], ps[:, pb * 512:pb * 512 + 256], AF.Relu,
                            bias=b1_sb[:, e_ * NH1 + ht:e_ * NH1 + ht + 1]),
                            wait=[SPE], self_wait=False)
                        relu_mark[ht] = SACT.n

                    def emit_w2(ht):
                        for tsub in range(2):
                            for jh in range(2):
                                wl = [(SACT, relu_mark[ht])] if (tsub == 0 and jh == 0) else []
                                if ht == 0 and tsub == 0 and jh == 0:
                                    wl.append((SDVE, last_acc[0]))
                                if tq == 0 and ht % 4 == 0 and tsub == 0 and jh == 0:
                                    wl.append((DW, w2g_mark[ht // 4]))
                                P.te(lambda e, ht=ht, tsub=tsub, jh=jh: e.matmul(
                                    bank(4 + tsub * 2 + jh),
                                    hb[ht % 2][:, tsub * 128:(tsub + 1) * 128],
                                    w2S[:, ht * D + jh * 512:ht * D + (jh + 1) * 512],
                                    start=(ht == 0), stop=False),
                                    wait=wl, inc=None)

                    for ht in range(NH1):
                        emit_w1(ht)
                        if ht >= 1:
                            emit_w2(ht - 1)
                    emit_w2(NH1 - 1)
                    for c in range(NCH):
                        w1c_reader[c] = SPE.n
                    for tsub in range(2):
                        for jh in range(2):
                            P.te(lambda e, tsub=tsub, jh=jh, e_=e_: e.matmul(
                                bank(4 + tsub * 2 + jh), ones_bf[0:1, :],
                                b2e[:, jh * 512:(jh + 1) * 512],
                                start=False, stop=True),
                                wait=[(DW, b2_mark)] if (tsub == 0 and jh == 0) else [],
                                inc=SPE if (tsub == 1 and jh == 1) else None)
                    if tq == 3:
                        b2_reader[0] = SPE.n
                        for gg in range(8):
                            w2g_reader[gg] = SPE.n
                    for tsub in range(2):
                        gt = tq * 2 + tsub
                        wcol = w_moe[:, (s * TT + half * 8 + gt) * NEXP + e_:
                                     (s * TT + half * 8 + gt) * NEXP + e_ + 1]
                        at = acc[:, gt * D:(gt + 1) * D]
                        for jh in range(2):
                            src2 = bank(4 + tsub * 2 + jh)
                            dst = at[:, jh * 512:(jh + 1) * 512]
                            if e_ == 0:
                                P.ve(lambda e, dst=dst, src=src2, wcol=wcol: e.tensor_scalar_mul(
                                    dst, src, wcol), wait=[SPE], inc=SDVE)
                            else:
                                P.ve(lambda e, dst=dst, src=src2, wcol=wcol: e.scalar_tensor_tensor(
                                    dst, src, wcol, dst, ALU.mult, ALU.add),
                                    wait=[SPE], inc=SDVE)
                    acc_mark[tq * 2] = SDVE.n
                    acc_mark[tq * 2 + 1] = SDVE.n
                    last_acc[0] = SDVE.n
            barrier()
            for i in range(8):
                tt = half * 8 + i
                P.sy(lambda e, s=s, tt=tt: e.dma_start(
                    out=gxe[:], in_=xmid[s, tt * 128:(tt + 1) * 128, :]),
                    wait=[SDVE] if i >= 1 else [], inc=DX)
                P.sy(lambda e, s=s, tt=tt: e.dma_start(
                    out=gse[:], in_=ssp[s, tt * 128:(tt + 1) * 128, :]), inc=DX)
                P.sc(lambda e, i=i: e.activation(ggb[:], acc[:, i * D:(i + 1) * D], AF.Silu),
                     wait=[dxv(), SDVE])
                P.ve(lambda e: e.tensor_mul(ggb[:], gse[:], ggb[:]), wait=[SACT], inc=SDVE)
                P.ve(lambda e: e.tensor_add(gob[:], gxe[:], ggb[:]),
                     wait=[(DS, DS.n)], inc=SDVE)
                P.sy(lambda e, s=s, tt=tt: e.dma_start(
                    out=y_out[s, tt * 128:(tt + 1) * 128, :], in_=gob[:]),
                    wait=[SDVE], inc=DS)
            barrier()
        A.pop()

    emit_setup()
    for l in (0, 1):
        load_layer_params(l)
        for s in range(B):
            A.push()
            xnT = A.alloc("xnT", [128, NCH * T], BF16)
            A.push()
            sw = {
                "kT": A.alloc("kT", [128, NCH * T], BF16),
                "v_sb": A.alloc("v_sb", [128, TT * H * (DH + 1)], BF16),
                "ub_rep": A.alloc("ub_rep", [128, 3 * D], BF16),
                "betak": A.alloc("betak", [128, TT * H], F32),
            }
            emit_phaseA(l, s, xnT, sw, dbgA=debug)
            if debug and l == 0:
                P.sy(lambda e, s=s, xnT=xnT: e.dma_start(out=dbg_xnt0[s], in_=xnT[:]), inc=DS)
                P.sy(lambda e, s=s, kt=sw["kT"]: e.dma_start(out=dbg_kt0[s], in_=kt[:]), inc=DS)
                P.sy(lambda e, s=s, vv=sw["v_sb"]: e.dma_start(out=dbg_v0[s], in_=vv[:]), inc=DS)
                barrier()
            emit_attn(l, s, xnT, sw, dbg=(dbg_qt0, dbg_at0) if (debug and l == 0) else None)
            A.pop()
            if l == 0:
                emit_ffn0(s, xnT, dbg=debug)
                if debug:
                    P.sy(lambda e, s=s: e.dma_start(out=dbg_s0[s], in_=ssp[s]), inc=DS)
                    barrier()
            A.pop()
        if l == 1:
            emit_ffn1()

    if debug:
        P.sy(lambda e: e.dma_start(out=dbg_xmid, in_=xmid), inc=DS)
    P.sy(lambda e: e.nop(), wait=[(DS, DS.n), (DX, DX.n), (DW, DW.n)])
    P.play()
    return nc


def _prep_inputs(inputs):
    f = np.ascontiguousarray

    def pm(vec):
        return f(vec.reshape(-1, 128).T)

    m = {}
    for l in (0, 1):
        p = f"L{l}_attn_"
        m[f"l{l}_qwT"] = f(inputs[p + "q_w"].T)
        m[f"l{l}_kwT"] = f(inputs[p + "k_w"].T)
        m[f"l{l}_vwT"] = f(inputs[p + "v_w"].T)
        m[f"l{l}_owT"] = f(inputs[p + "o_w"].T)
        m[f"l{l}_g_q"] = pm(inputs[p + "ln_q_g"])
        m[f"l{l}_g_k"] = pm(inputs[p + "ln_k_g"])
        m[f"l{l}_g_v"] = pm(inputs[p + "ln_v_g"])
        m[f"l{l}_b_q"] = pm(inputs[p + "ln_q_b"])
        m[f"l{l}_b_k"] = pm(inputs[p + "ln_k_b"])
        m[f"l{l}_b_v"] = pm(inputs[p + "ln_v_b"])
        m[f"l{l}_qb"] = f(inputs[p + "q_b"].reshape(1, -1))
        m[f"l{l}_kb"] = f(inputs[p + "k_b"].reshape(1, -1))
        m[f"l{l}_vb"] = f(inputs[p + "v_b"].reshape(1, -1))
        m[f"l{l}_ob"] = f(inputs[p + "o_b"].reshape(1, -1))
        m[f"l{l}_beta"] = f(inputs[p + "beta_param"].reshape(1, -1))
    m["wg_t"] = f(inputs["L0_ffn_w_gate"].T)
    m["wu_t"] = f(inputs["L0_ffn_w_up"].T)
    m["wd_t"] = f(inputs["L0_ffn_w_down"].T)
    m["rwg"] = f(inputs["L1_ffn_Wg"])
    m["rbg"] = f(inputs["L1_ffn_bg"].reshape(1, -1))
    m["w1"] = f(inputs["L1_ffn_W1"])
    b1v = inputs["L1_ffn_b1"]
    m["b1"] = f(b1v.reshape(NEXP, NH1, 128).transpose(2, 0, 1).reshape(128, -1))
    m["w2"] = f(inputs["L1_ffn_W2"])
    m["b2"] = f(inputs["L1_ffn_b2"].reshape(1, -1))
    return m


def kernel(**inputs):
    x = np.ascontiguousarray(inputs["L0_attn_query"])
    shared = _prep_inputs(inputs)
    nc = build()
    in_maps = []
    for c in range(8):
        im = dict(shared)
        im["x"] = np.ascontiguousarray(x[c * B:(c + 1) * B])
        in_maps.append(im)
    res = run_bass_kernel_spmd(nc, in_maps, core_ids=list(range(8)))
    return np.concatenate([r["y"] for r in res.results], axis=0)
